# version 01
# baseline (speedup 1.0000x reference)
import math
import numpy as np
import ml_dtypes
from contextlib import ExitStack

import concourse.bass as bass
import concourse.mybir as mybir
from concourse.bass_utils import run_bass_kernel_spmd

F32 = mybir.dt.float32
BF16 = mybir.dt.bfloat16
AF = mybir.ActivationFunctionType
ALU = mybir.AluOpType
AX = mybir.AxisListType

NCORE = 8
NSEQ = 4
T = 2048
D = 1024
NT = T // 128
ND = D // 128
NEG_BIG = -1e30
HEAD_AT_SLOT = [0, 4, 1, 5, 2, 6, 3, 7, 8, 12, 9, 13, 10, 14, 11, 15]

ENGINES = ("tensor", "vector", "scalar", "gpsimd", "sync")


class Sem:
    __slots__ = ("name", "h", "n")

    def __init__(self, name, handle):
        self.name = name
        self.h = handle
        self.n = 0


class Prog:
    def __init__(self, nc, es):
        self.nc = nc
        self.es = es
        self.ops = {e: [] for e in ENGINES}
        self._sems = []

    def sem(self, name):
        s = Sem(name, self.es.enter_context(self.nc.semaphore(name)))
        self._sems.append(s)
        return s

    def emit(self, eng, fn, wait=(), inc=(), drain=False):
        incs = []
        for s, a in inc:
            s.n += a
            assert s.n < 60000, f"sem {s.name} overflow {s.n}"
            incs.append((s, a))
        self.ops[eng].append((fn, [(s, v) for (s, v) in wait if v > 0], incs, drain))

    def dma(self, eng, out, in_, sem, wait=(), **kw):
        def fn(h):
            return h.dma_start(out=out, in_=in_, **kw)
        self.emit(eng, fn, wait=wait, inc=[(sem, 16)])
        return sem.n

    def barrier(self, sems):
        waits = [(s, s.n) for s in sems if s.n > 0]
        for eng in ENGINES:
            self.emit(eng, lambda h: None, wait=waits)

    def build(self):
        block_cm = self.nc.Block()
        block = block_cm.__enter__()
        for eng in ENGINES:
            ops = self.ops[eng]
            if not ops:
                continue

            def body(h, ops=ops):
                for fn, waits, incs, drain in ops:
                    for s, v in waits:
                        h.wait_ge(s.h, v)
                    inst = fn(h)
                    if drain:
                        inst = h.drain()
                    if incs:
                        assert inst is not None
                        for s, a in incs:
                            inst.then_inc(s.h, a)
            getattr(block, eng)(body)
        block_cm.__exit__(None, None, None)



def rope_tables(base, n=T, hd=64):
    inv = 1.0 / base ** (np.arange(0, hd, 2, dtype=np.float64) / hd)
    t = np.arange(n, dtype=np.float64)
    fr = np.outer(t, inv)
    return np.cos(fr).astype(np.float32), np.sin(fr).astype(np.float32)


def host_consts():
    c = {}
    c["ident"] = np.eye(128, dtype=np.float32)
    c["ones_row"] = np.ones((1, 128), np.float32)
    cos0, sin0 = rope_tables(1e4)
    cosg, sing = rope_tables(1e6)
    c["cos_l"] = cos0
    c["sin_l"] = sin0
    c["cos_g"] = cosg
    c["sin_g"] = sing
    i = np.arange(128)
    tri_incl = np.where(i[:, None] >= i[None, :], 0.0, NEG_BIG).astype(np.float32)
    tri_excl = np.where(i[None, :] > i[:, None], 0.0, NEG_BIG).astype(np.float32)
    c["tri_incl"] = tri_incl
    c["tri_excl"] = tri_excl
    return c



def ln_stats_and_apply(p, nc, C, xtile_ap, t,
                       sm, svar, srr, sr1,
                       xc_scr, sq_scr, tiny, sem_V, sem_S,
                       eps1=1e-5, double_ln=True, rms_second=False,
                       rms_eps=1e-6):
    ALU_ = ALU
    def f_sum(v, xa=xtile_ap, tt=t):
        v.reduce_sum(tiny[:, 0:1], xa, axis=AX.X)
        v.drain()
        v.tensor_scalar(sm[:, tt:tt + 1], tiny[:, 0:1], 1.0 / D, None, ALU_.mult)
        v.drain()
        return v.tensor_scalar(xc_scr, xa, sm[:, tt:tt + 1], None, ALU_.subtract)
    p.emit("vector", f_sum, wait=C.pop("w_stats", []), inc=[(sem_V, 1)], drain=True)
    vc = sem_V.n
    def f_sq(s, tt=t):
        return s.activation(sq_scr, xc_scr, AF.Square, accum_out=svar[:, tt:tt + 1])
    p.emit("scalar", f_sq, wait=[(sem_V, vc)], inc=[(sem_S, 1)], drain=True)
    sc = sem_S.n
    def f_var(v, tt=t):
        v.tensor_scalar(tiny[:, 0:1], svar[:, tt:tt + 1], 1.0 / D, eps1,
                        ALU_.mult, ALU_.add)
        v.drain()
        v.reciprocal(tiny[:, 1:2], tiny[:, 0:1])
        if double_ln:
            e2 = rms_eps if rms_second else eps1
            v.tensor_scalar(tiny[:, 4:5], svar[:, tt:tt + 1], 1.0 / D, None,
                            ALU_.mult)
            v.drain()
            v.tensor_tensor(tiny[:, 5:6], tiny[:, 4:5], tiny[:, 1:2], ALU_.mult)
            v.drain()
            v.tensor_scalar(tiny[:, 5:6], tiny[:, 5:6], 1.0, e2,
                            ALU_.mult, ALU_.add)
            v.drain()
            v.reciprocal(tiny[:, 6:7], tiny[:, 5:6])
        return v.nop()
    p.emit("vector", f_var, wait=[(sem_S, sc)], inc=[(sem_V, 1)], drain=True)
    vc2 = sem_V.n
    def f_rt(s, tt=t):
        s.activation(sr1[:, tt:tt + 1], tiny[:, 1:2], AF.Sqrt)
        if double_ln:
            return s.activation(tiny[:, 7:8], tiny[:, 6:7], AF.Sqrt)
        return s.copy(tiny[:, 7:8], sr1[:, tt:tt + 1])
    p.emit("scalar", f_rt, wait=[(sem_V, vc2)], inc=[(sem_S, 1)], drain=True)
    sc2 = sem_S.n
    def f_rr(v, tt=t):
        if double_ln:
            return v.tensor_tensor(srr[:, tt:tt + 1], sr1[:, tt:tt + 1],
                                   tiny[:, 7:8], ALU_.mult)
        return v.tensor_copy(srr[:, tt:tt + 1], sr1[:, tt:tt + 1])
    p.emit("vector", f_rr, wait=[(sem_S, sc2)], inc=[(sem_V, 1)], drain=True)
    return sem_V.n



def build_l0attn(nseq=NSEQ):
    nc = bass.Bass("TRN2", target_bir_lowering=False)
    xin = nc.dram_tensor("xin", [nseq, T, D], F32, kind="ExternalInput").ap()
    wqkv = nc.dram_tensor("wqkv", [D, 3 * D], F32, kind="ExternalInput").ap()
    bqkv = nc.dram_tensor("bqkv", [1, 3 * D], F32, kind="ExternalInput").ap()
    wo = nc.dram_tensor("wo", [D, D], F32, kind="ExternalInput").ap()
    bo = nc.dram_tensor("bo", [1, D], F32, kind="ExternalInput").ap()
    lnw = nc.dram_tensor("lnw", [1, D], F32, kind="ExternalInput").ap()
    lnb = nc.dram_tensor("lnb", [1, D], F32, kind="ExternalInput").ap()
    ident = nc.dram_tensor("ident", [128, 128], F32, kind="ExternalInput").ap()
    cos_l = nc.dram_tensor("cos_l", [T, 32], F32, kind="ExternalInput").ap()
    sin_l = nc.dram_tensor("sin_l", [T, 32], F32, kind="ExternalInput").ap()
    ones_host = nc.dram_tensor("ones_row", [1, 128], F32, kind="ExternalInput").ap()
    xout = nc.dram_tensor("xout", [nseq, T, D], F32, kind="ExternalOutput").ap()

    es = ExitStack()
    En = es.enter_context
    p = Prog(nc, es)

    qhm = En(nc.sbuf_tensor("qhm", [128, ND * T], BF16))
    khm = En(nc.sbuf_tensor("khm", [128, ND * T], BF16))
    vtm = En(nc.sbuf_tensor("vtm", [128, NT * D], BF16))
    wbig = En(nc.sbuf_tensor("wbig", [128, ND * 3 * D], BF16))
    bqkv_sb = En(nc.sbuf_tensor("bqkv_sb", [1, 3 * D], BF16))
    bo_sb = En(nc.sbuf_tensor("bo_sb", [1, D], BF16))
    ones_sb = En(nc.sbuf_tensor("ones_sb", [1, 128], BF16))
    lnw_sb = En(nc.sbuf_tensor("lnw_sb", [128, D], F32))
    lnb_sb = En(nc.sbuf_tensor("lnb_sb", [128, D], F32))
    id_sb = En(nc.sbuf_tensor("id_sb", [128, 128], BF16))
    cos_sb = En(nc.sbuf_tensor("cos_sb", [128, NT * 32], F32))
    sin_sb = En(nc.sbuf_tensor("sin_sb", [128, NT * 32], F32))
    xtile = [En(nc.sbuf_tensor(f"xtile{i}", [128, D], F32)) for i in range(2)]
    xc_scr = En(nc.sbuf_tensor("xc_scr", [128, D], F32))
    sq_scr = En(nc.sbuf_tensor("sq_scr", [128, D], F32))
    xn_scr = En(nc.sbuf_tensor("xn_scr", [128, D], BF16))
    xnt_scr = En(nc.sbuf_tensor("xnt_scr", [128, ND * 128], BF16))
    qkvt = En(nc.sbuf_tensor("qkvt", [128, 2 * D], F32))
    qrot = En(nc.sbuf_tensor("qrot", [128, D], BF16))
    krot = En(nc.sbuf_tensor("krot", [128, D], BF16))
    sm = En(nc.sbuf_tensor("sm", [128, NT], F32))
    svar = En(nc.sbuf_tensor("svar", [128, NT], F32))
    srr = En(nc.sbuf_tensor("srr", [128, NT], F32))
    sr1 = En(nc.sbuf_tensor("sr1", [128, NT], F32))
    tiny = En(nc.sbuf_tensor("tiny", [128, 8], F32))
    ep_tiny = En(nc.sbuf_tensor("ep_tiny", [128, 8], F32))
    P_sb = En(nc.sbuf_tensor("P_sb", [128, T], BF16))
    PT_sb = En(nc.sbuf_tensor("PT_sb", [128, T], BF16))
    rmax4 = En(nc.sbuf_tensor("rmax4", [128, 4], F32))
    nmax = En(nc.sbuf_tensor("nmax", [128, 1], F32))
    ssum4 = En(nc.sbuf_tensor("ssum4", [128, 4], F32))
    rsum = En(nc.sbuf_tensor("rsum", [128, 1], F32))
    afm = En(nc.sbuf_tensor("afm", [128, ND * 128], BF16))
    z_scr = [xc_scr, sq_scr]
    y0f_scr = qkvt

    PA = En(nc.psum_tensor("PA", [128, T], F32))
    PB = En(nc.psum_tensor("PB", [128, D], F32))
    PT2 = [En(nc.psum_tensor(f"PT2{i}", [128, 512], BF16))
           for i in range(2)]

    sem_w = p.sem("w")
    sem_xl = [p.sem(f"xl{i}") for i in range(2)]
    sem_xs = [p.sem(f"xs{i}") for i in range(2)]
    sem_T = p.sem("T")
    sem_V = p.sem("V")
    sem_S = p.sem("S")

    p.dma("gpsimd", bqkv_sb[:], bqkv, sem_w)
    p.dma("gpsimd", bo_sb[:], bo, sem_w)
    p.dma("gpsimd", lnw_sb[:], lnw.broadcast_to([128, D]), sem_w)
    p.dma("gpsimd", lnb_sb[:], lnb.broadcast_to([128, D]), sem_w)
    p.dma("gpsimd", id_sb[:], ident, sem_w)
    p.dma("gpsimd", cos_sb[:].rearrange("p (n f) -> p n f", f=32),
          cos_l.rearrange("(n p) f -> p n f", p=128), sem_w)
    p.dma("gpsimd", sin_sb[:].rearrange("p (n f) -> p n f", f=32),
          sin_l.rearrange("(n p) f -> p n f", p=128), sem_w)
    p.dma("gpsimd", ones_sb[:], ones_host, sem_w)

    C = {}

    for s in range(nseq):
        ww = [(sem_T, C.get(("wbig_free",), 0)), (sem_w, sem_w.n)]
        p.dma("gpsimd", wbig[:].rearrange("p (c o) -> p c o", o=3 * D),
              wqkv.rearrange("(c p) o -> p c o", p=128), sem_w,
              wait=[w for w in ww if w[1] > 0])
        wqkv_ready = sem_w.n

        for t in range(NT):
            par = t % 2
            lwait = []
            if ("xtile_free", par) in C:
                lwait = [(sem_V, C[("xtile_free", par)])]
            p.dma("sync", xtile[par][:], xin[s, t * 128:(t + 1) * 128, :],
                  sem_xl[par], wait=lwait)
            xl_cnt = sem_xl[par].n

            C["w_stats"] = [(sem_xl[par], xl_cnt)]
            if t == 0:
                for pp in range(2):
                    if ("zscr_free", pp) in C:
                        C["w_stats"].append((sem_xs[pp], C[("zscr_free", pp)]))
            rr_ready = ln_stats_and_apply(
                p, nc, C, xtile[par][:], t, sm, svar, srr, sr1,
                xc_scr[:], sq_scr[:], tiny, sem_V, sem_S,
                double_ln=True, rms_second=False)

            def f_apply(v, tt=t):
                v.tensor_scalar(xc_scr[:], xc_scr[:], srr[:, tt:tt + 1],
                                None, ALU.mult)
                v.drain()
                v.tensor_tensor(xc_scr[:], xc_scr[:], lnw_sb[:], ALU.mult)
                v.drain()
                return v.tensor_tensor(xn_scr[:], xc_scr[:], lnb_sb[:], ALU.add)
            aw = [(sem_V, rr_ready), (sem_w, sem_w.n)]
            if ("xn_free",) in C:
                aw.append((sem_T, C[("xn_free",)]))
            p.emit("vector", f_apply, wait=aw, inc=[(sem_V, 1)], drain=True)
            C[("xtile_free", par)] = sem_V.n
            apply_done = sem_V.n

            for g in range(2):
                bk = g % 2
                tw = [(sem_V, apply_done)] if g == 0 else []
                if ("ptbank_free", bk) in C:
                    tw.append((sem_S, C[("ptbank_free", bk)]))
                for j in range(4):
                    c = g * 4 + j
                    def f_tr(cc=c, jj=j, bb=bk):
                        def f(te):
                            return nc.tensor.transpose(
                                PT2[bb][:, jj * 128:(jj + 1) * 128],
                                xn_scr[:, cc * 128:(cc + 1) * 128], id_sb[:])
                        return f
                    p.emit("tensor", f_tr(), wait=tw if j == 0 else [],
                           inc=[(sem_T, 1)] if j == 3 else [])
                    tw = []
                trc = sem_T.n
                def f_ev(gg=g, bb=bk):
                    def f(se):
                        return se.copy(xnt_scr[:, gg * 512:(gg + 1) * 512],
                                       PT2[bb][:])
                    return f
                p.emit("scalar", f_ev(), wait=[(sem_T, trc)],
                       inc=[(sem_S, 1)], drain=True)
                C[("ptbank_free", bk)] = sem_S.n
            xnt_ready = sem_S.n
            C[("xn_free",)] = sem_T.n

            qkv_evict_done = {}
            for ob in range(6):
                bank = ob % 2
                pslice = PA[:, bank * 512:(bank + 1) * 512]
                mw = [(sem_S, xnt_ready), (sem_w, wqkv_ready)] if ob == 0 else []
                if ("pa_free", bank) in C:
                    mw.append((sem_S, C[("pa_free", bank)]))
                for c in range(ND):
                    def f_mm(cc=c, oo=ob, ps=pslice):
                        def f(te):
                            return nc.tensor.matmul(
                                ps, xnt_scr[:, cc * 128:(cc + 1) * 128],
                                wbig[:, cc * 3 * D + oo * 512:
                                     cc * 3 * D + oo * 512 + 512],
                                start=(cc == 0), stop=False)
                        return f
                    p.emit("tensor", f_mm(), wait=mw if c == 0 else [])
                    mw = []
                def f_bias(oo=ob, ps=pslice):
                    def f(te):
                        return nc.tensor.matmul(
                            ps, ones_sb[0:1, :],
                            bqkv_sb[0:1, oo * 512:(oo + 1) * 512],
                            start=False, stop=True)
                    return f
                p.emit("tensor", f_bias(), inc=[(sem_T, 1)])
                mmc = sem_T.n
                if ob < 4:
                    def f_qe(oo=ob, ps=pslice):
                        def f(se):
                            return se.copy(qkvt[:, oo * 512:(oo + 1) * 512], ps)
                        return f
                else:
                    def f_qe(oo=ob, ps=pslice, tt=t):
                        def f(se):
                            return se.copy(
                                vtm[:, tt * D + (oo - 4) * 512:
                                    tt * D + (oo - 4) * 512 + 512], ps)
                        return f
                ew = [(sem_T, mmc)]
                if ob == 0 and ("qkvt_free",) in C:
                    ew.append((sem_V, C[("qkvt_free",)]))
                if ob == 4 and ("vtm_free",) in C:
                    ew.append((sem_T, C[("vtm_free",)]))
                p.emit("scalar", f_qe(), wait=ew, inc=[(sem_S, 1)], drain=True)
                C[("pa_free", bank)] = sem_S.n
                qkv_evict_done[ob] = sem_S.n

            def rope_emit(src_off, dst, tt, nh=16):
                def f(v):
                    base = qkvt[:, src_off:src_off + nh * 64].rearrange(
                        "p (h f two) -> p h f two", two=2, f=32)
                    e = base[:, :, :, 0:1].squeeze(3)
                    o = base[:, :, :, 1:2].squeeze(3)
                    cosv = cos_sb[:, tt * 32:(tt + 1) * 32].unsqueeze(1) \
                        .broadcast_to([128, nh, 32])
                    sinv = sin_sb[:, tt * 32:(tt + 1) * 32].unsqueeze(1) \
                        .broadcast_to([128, nh, 32])
                    n = nh * 32
                    t1 = xc_scr[:, 0:n].rearrange("p (h f) -> p h f", f=32)
                    t2 = sq_scr[:, 0:n].rearrange("p (h f) -> p h f", f=32)
                    t3 = xc_scr[:, n:2 * n].rearrange("p (h f) -> p h f", f=32)
                    t4 = sq_scr[:, n:2 * n].rearrange("p (h f) -> p h f", f=32)
                    dbase = dst[:, 0:nh * 64].rearrange(
                        "p (h f two) -> p h f two", two=2, f=32)
                    de = dbase[:, :, :, 0:1].squeeze(3)
                    do = dbase[:, :, :, 1:2].squeeze(3)
                    v.tensor_tensor(t1, e, cosv, ALU.mult)
                    v.tensor_tensor(t2, o, sinv, ALU.mult)
                    v.drain()
                    v.tensor_tensor(de, t1, t2, ALU.subtract)
                    v.tensor_tensor(t3, e, sinv, ALU.mult)
                    v.tensor_tensor(t4, o, cosv, ALU.mult)
                    v.drain()
                    return v.tensor_tensor(do, t3, t4, ALU.add)
                return f
            rw = [(sem_S, qkv_evict_done[1])]
            if ("rot_free", 0) in C:
                rw.append((sem_T, C[("rot_free", 0)]))
            p.emit("vector", rope_emit(0, qrot[:], t), wait=rw,
                   inc=[(sem_V, 1)], drain=True)
            q_rot_done = sem_V.n
            rw = [(sem_S, qkv_evict_done[3])]
            if ("rot_free", 1) in C:
                rw.append((sem_T, C[("rot_free", 1)]))
            p.emit("vector", rope_emit(D, krot[:], t), wait=rw,
                   inc=[(sem_V, 1)], drain=True)
            k_rot_done = sem_V.n
            C[("qkvt_free",)] = sem_V.n

            for (rot, dstb, rfree, rdone) in ((qrot, qhm, 0, q_rot_done),
                                              (krot, khm, 1, k_rot_done)):
                for g in range(2):
                    bk = g % 2
                    tw = [(sem_V, rdone)] if g == 0 else []
                    if ("ptbank_free", bk) in C:
                        tw.append((sem_S, C[("ptbank_free", bk)]))
                    for j in range(4):
                        c = g * 4 + j
                        def f_tr2(cc=c, jj=j, bb=bk, rt=rot):
                            def f(te):
                                return nc.tensor.transpose(
                                    PT2[bb][:, jj * 128:(jj + 1) * 128],
                                    rt[:, cc * 128:(cc + 1) * 128], id_sb[:])
                            return f
                        p.emit("tensor", f_tr2(), wait=tw if j == 0 else [],
                               inc=[(sem_T, 1)] if j == 3 else [])
                        tw = []
                    trc = sem_T.n
                    def f_ev2(gg=g, bb=bk, db=dstb, tt=t):
                        def f(se):
                            dst3 = db[:].rearrange("p (c t) -> p c t", t=T)[
                                :, gg * 4:(gg + 1) * 4,
                                tt * 128:(tt + 1) * 128]
                            return se.copy(dst3, PT2[bb][:].rearrange(
                                "p (j t) -> p j t", t=128))
                        return f
                    p.emit("scalar", f_ev2(), wait=[(sem_T, trc)],
                           inc=[(sem_S, 1)], drain=True)
                    C[("ptbank_free", bk)] = sem_S.n
                C[("rot_free", rfree)] = sem_T.n
        qkv_all_done = sem_S.n
        C[("vtm_free",)] = 0

        wow = [(sem_T, sem_T.n), (sem_w, sem_w.n)]
        p.dma("gpsimd", wbig[:].rearrange("p (c o) -> p c o", o=3 * D)
              [:, :, 0:D], wo.rearrange("(c p) o -> p c o", p=128), sem_w,
              wait=wow)
        wo_ready = sem_w.n

        for qt in range(NT):
            for h in range(16):
                ch, hf = h // 2, (h % 2) * 64
                sw = [(sem_S, qkv_all_done)] if (qt == 0 and h == 0) else []
                if ("s_free",) in C:
                    sw.append((sem_S, C[("s_free",)]))
                for kb in range(4):
                    def f_s(kbb=kb, chh=ch, hff=hf, qtt=qt):
                        def f(te):
                            return nc.tensor.matmul(
                                PA[:, kbb * 512:(kbb + 1) * 512],
                                qhm[hff:hff + 64,
                                    chh * T + qtt * 128:chh * T + qtt * 128 + 128],
                                khm[hff:hff + 64,
                                    chh * T + kbb * 512:chh * T + (kbb + 1) * 512],
                                start=True, stop=True)
                        return f
                    p.emit("tensor", f_s(), wait=sw if kb == 0 else [],
                           inc=[(sem_T, 1)] if kb == 3 else [])
                    sw = []
                s_done = sem_T.n
                def f_max(v):
                    for kb in range(4):
                        v.reduce_max(rmax4[:, kb:kb + 1],
                                     PA[:, kb * 512:(kb + 1) * 512], axis=AX.X)
                    v.drain()
                    v.reduce_max(nmax[:, 0:1], rmax4[:], axis=AX.X)
                    v.drain()
                    return v.tensor_scalar(nmax[:, 0:1], nmax[:, 0:1], -0.125,
                                           None, ALU.mult)
                mw = [(sem_T, s_done)]
                if ("nmax_free",) in C:
                    mw.append((sem_S, C[("nmax_free",)]))
                p.emit("vector", f_max, wait=mw, inc=[(sem_V, 1)], drain=True)
                max_done = sem_V.n
                def f_exp(s_):
                    inst = None
                    for kb in range(4):
                        inst = s_.activation(
                            P_sb[:, kb * 512:(kb + 1) * 512],
                            PA[:, kb * 512:(kb + 1) * 512], AF.Exp,
                            scale=0.125, bias=nmax[:, 0:1],
                            accum_out=ssum4[:, kb:kb + 1])
                    return inst
                ew = [(sem_V, max_done)]
                if ("p_free",) in C:
                    ew.append((sem_T, C[("p_free",)]))
                p.emit("scalar", f_exp, wait=ew, inc=[(sem_S, 1)], drain=True)
                C[("s_free",)] = sem_S.n
                C[("nmax_free",)] = sem_S.n
                exp_done = sem_S.n
                def f_norm(v):
                    v.reduce_sum(rsum[:, 0:1], ssum4[:], axis=AX.X)
                    v.drain()
                    v.reciprocal(rsum[:, 0:1], rsum[:, 0:1])
                    v.drain()
                    return v.tensor_scalar(P_sb[:], P_sb[:], rsum[:, 0:1],
                                           None, ALU.mult)
                p.emit("vector", f_norm, wait=[(sem_S, exp_done)],
                       inc=[(sem_V, 1)], drain=True)
                norm_done = sem_V.n
                for g in range(4):
                    bk = g % 2
                    tw = [(sem_V, norm_done)] if g == 0 else []
                    if ("ptbank_free", bk) in C:
                        tw.append((sem_S, C[("ptbank_free", bk)]))
                    for j in range(4):
                        kc = g * 4 + j
                        def f_pt(kcc=kc, bb=bk, jj=j):
                            def f(te):
                                return nc.tensor.transpose(
                                    PT2[bb][:, jj * 128:(jj + 1) * 128],
                                    P_sb[:, kcc * 128:(kcc + 1) * 128], id_sb[:])
                            return f
                        p.emit("tensor", f_pt(), wait=tw if j == 0 else [],
                               inc=[(sem_T, 1)] if j == 3 else [])
                        tw = []
                    g_done = sem_T.n
                    def f_pte(gg=g, bb=bk):
                        def f(se):
                            return se.copy(PT_sb[:, gg * 512:(gg + 1) * 512],
                                           PT2[bb][:])
                        return f
                    p.emit("scalar", f_pte(), wait=[(sem_T, g_done)],
                           inc=[(sem_S, 1)], drain=True)
                    C[("ptbank_free", bk)] = sem_S.n
                pt_done = sem_S.n
                C[("p_free",)] = sem_T.n
                aw2 = [(sem_S, pt_done)]
                if ("po_free",) in C:
                    aw2.append((sem_S, C[("po_free",)]))
                if h == 0 and ("pb_free",) in C:
                    aw2.append((sem_S, C[("pb_free",)]))
                for kc in range(16):
                    def f_av(kcc=kc, hh=h):
                        def f(te):
                            return nc.tensor.matmul(
                                PB[0:64, 0:128],
                                vtm[:, kcc * D + hh * 64:kcc * D + hh * 64 + 64],
                                PT_sb[:, kcc * 128:(kcc + 1) * 128],
                                start=(kcc == 0), stop=(kcc == 15))
                        return f
                    p.emit("tensor", f_av(), wait=aw2 if kc == 0 else [],
                           inc=[(sem_T, 1)] if kc == 15 else [])
                    aw2 = []
                av_done = sem_T.n
                def f_ave(hh=h):
                    ch2, hf2 = hh // 2, (hh % 2) * 64
                    def f(se):
                        return se.copy(
                            afm[hf2:hf2 + 64, ch2 * 128:(ch2 + 1) * 128],
                            PB[0:64, 0:128])
                    return f
                avw = [(sem_T, av_done)]
                if h == 0 and ("afm_free",) in C:
                    avw.append((sem_T, C[("afm_free",)]))
                p.emit("scalar", f_ave(), wait=avw, inc=[(sem_S, 1)], drain=True)
                C[("po_free",)] = sem_S.n
            heads_done = sem_S.n

            zw = [(sem_S, heads_done), (sem_w, wo_ready)]
            if ("pb_free",) in C:
                zw.append((sem_S, C[("pb_free",)]))
            for ob in range(2):
                for c in range(ND):
                    def f_wo(cc=c, oo=ob):
                        def f(te):
                            return nc.tensor.matmul(
                                PB[:, oo * 512:(oo + 1) * 512],
                                afm[:, cc * 128:(cc + 1) * 128],
                                wbig[:, cc * 3 * D + oo * 512:
                                     cc * 3 * D + oo * 512 + 512],
                                start=(cc == 0), stop=False)
                        return f
                    p.emit("tensor", f_wo(), wait=zw)
                    zw = []
                def f_bo(oo=ob):
                    def f(te):
                        return nc.tensor.matmul(
                            PB[:, oo * 512:(oo + 1) * 512], ones_sb[0:1, :],
                            bo_sb[0:1, oo * 512:(oo + 1) * 512],
                            start=False, stop=True)
                    return f
                p.emit("tensor", f_bo(), inc=[(sem_T, 1)] if ob == 1 else [])
            wo_done = sem_T.n
            C[("afm_free",)] = sem_T.n
            C[("wbig_free",)] = sem_T.n

            par = qt % 2
            lw = [(sem_V, C.get(("xtile_free", par), 0))]
            p.dma("sync", xtile[par][:], xin[s, qt * 128:(qt + 1) * 128, :],
                  sem_xl[par], wait=[w for w in lw if w[1] > 0])
            xrl = sem_xl[par].n
            def f_ep1(par_=par):
                def f(se):
                    return se.copy(z_scr[par_][:], PB[:])
                return f
            ew = [(sem_T, wo_done)]
            if ("zscr_free", par) in C:
                ew.append((sem_xs[par], C[("zscr_free", par)]))
            p.emit("scalar", f_ep1(), wait=ew, inc=[(sem_S, 1)], drain=True)
            C[("pb_free",)] = sem_S.n
            ep1 = sem_S.n
            def f_ep2(par_=par, qtt=qt):
                def f(v):
                    v.tensor_scalar(y0f_scr[:, 0:D], xtile[par_][:],
                                    sm[:, qtt:qtt + 1], sr1[:, qtt:qtt + 1],
                                    ALU.subtract, ALU.mult)
                    v.drain()
                    v.tensor_tensor(z_scr[par_][:], z_scr[par_][:],
                                    y0f_scr[:, 0:D], ALU.add)
                    v.drain()
                    v.reduce_sum(ep_tiny[:, 0:1], z_scr[par_][:], axis=AX.X)
                    v.drain()
                    v.tensor_scalar(ep_tiny[:, 1:2], ep_tiny[:, 0:1], 1.0 / D,
                                    None, ALU.mult)
                    v.drain()
                    return v.tensor_scalar(z_scr[par_][:], z_scr[par_][:],
                                           ep_tiny[:, 1:2], None, ALU.subtract)
                return f
            p.emit("vector", f_ep2(), wait=[(sem_S, ep1), (sem_xl[par], xrl)],
                   inc=[(sem_V, 1)], drain=True)
            C[("qkvt_free",)] = sem_V.n
            ep2 = sem_V.n
            def f_ep3(par_=par):
                def f(se):
                    return se.activation(y0f_scr[:, 0:D], z_scr[par_][:],
                                         AF.Square, accum_out=ep_tiny[:, 2:3])
                return f
            p.emit("scalar", f_ep3(), wait=[(sem_V, ep2)], inc=[(sem_S, 1)],
                   drain=True)
            ep3 = sem_S.n
            def f_ep4(v):
                v.tensor_scalar(ep_tiny[:, 3:4], ep_tiny[:, 2:3], 1.0 / D, 1e-5,
                                ALU.mult, ALU.add)
                v.drain()
                return v.reciprocal(ep_tiny[:, 4:5], ep_tiny[:, 3:4])
            p.emit("vector", f_ep4, wait=[(sem_S, ep3)], inc=[(sem_V, 1)],
                   drain=True)
            ep4 = sem_V.n
            def f_ep5(se):
                return se.activation(ep_tiny[:, 5:6], ep_tiny[:, 4:5], AF.Sqrt)
            p.emit("scalar", f_ep5, wait=[(sem_V, ep4)], inc=[(sem_S, 1)],
                   drain=True)
            ep5 = sem_S.n
            def f_ep6(par_=par):
                def f(v):
                    v.tensor_scalar(z_scr[par_][:], z_scr[par_][:],
                                    ep_tiny[:, 5:6], None, ALU.mult)
                    v.drain()
                    return v.tensor_tensor(z_scr[par_][:], xtile[par_][:],
                                           z_scr[par_][:], ALU.add)
                return f
            p.emit("vector", f_ep6(), wait=[(sem_S, ep5)], inc=[(sem_V, 1)],
                   drain=True)
            C[("xtile_free", par)] = sem_V.n
            outd = sem_V.n
            p.dma("sync", xout[s, qt * 128:(qt + 1) * 128, :], z_scr[par][:],
                  sem_xs[par], wait=[(sem_V, outd)])
            C[("zscr_free", par)] = sem_xs[par].n
        C[("vtm_free",)] = sem_T.n

    p.emit("sync", lambda h: None,
           wait=[(sem_xs[0], sem_xs[0].n), (sem_xs[1], sem_xs[1].n)])
    p.build()
    es.close()
    return nc



def build_l0ffn(nseq=NSEQ):
    NE = 17
    ROUTE_SCALE = 2.826
    nc = bass.Bass("TRN2", target_bir_lowering=False)
    xin = nc.dram_tensor("xin", [nseq, T, D], F32, kind="ExternalInput").ap()
    wg_all = nc.dram_tensor("wg_all", [NE, D, D], F32, kind="ExternalInput").ap()
    wu_all = nc.dram_tensor("wu_all", [NE, D, D], F32, kind="ExternalInput").ap()
    wd_all = nc.dram_tensor("wd_all", [NE, D, D], F32, kind="ExternalInput").ap()
    gwT = nc.dram_tensor("gwT", [D, 16], F32, kind="ExternalInput").ap()
    ebias = nc.dram_tensor("ebias", [1, 16], F32, kind="ExternalInput").ap()
    ident = nc.dram_tensor("ident", [128, 128], F32, kind="ExternalInput").ap()
    xout = nc.dram_tensor("xout", [nseq, T, D], F32, kind="ExternalOutput").ap()

    es = ExitStack()
    En = es.enter_context
    p = Prog(nc, es)

    y1fm = En(nc.sbuf_tensor("y1fm", [128, ND * T], BF16))
    yacc = En(nc.sbuf_tensor("yacc", [128, NT * D], F32))
    wgb = [En(nc.sbuf_tensor(f"wgb{i}", [128, ND * D], BF16)) for i in range(2)]
    wub = [En(nc.sbuf_tensor(f"wub{i}", [128, ND * D], BF16)) for i in range(2)]
    wdb = En(nc.sbuf_tensor("wdb", [128, ND * D], BF16))
    hblk = En(nc.sbuf_tensor("hblk", [128, ND * 512], BF16))
    sil = [En(nc.sbuf_tensor(f"sil{i}", [128, 512], F32)) for i in range(2)]
    xtile = [En(nc.sbuf_tensor(f"xt{i}", [128, D], F32)) for i in range(2)]
    y1t_scr = En(nc.sbuf_tensor("y1t_scr", [128, D], BF16))
    gw_sb = En(nc.sbuf_tensor("gw_sb", [128, 16], BF16))
    gw_sb2 = En(nc.sbuf_tensor("gw_sb2", [128, ND * 16], BF16))
    eb_sb = En(nc.sbuf_tensor("eb_sb", [128, 16], F32))
    id_sb = En(nc.sbuf_tensor("id_sb", [128, 128], BF16))
    scores = En(nc.sbuf_tensor("scores", [128, NT * 16], F32))
    combine = En(nc.sbuf_tensor("combine", [128, NT * 16], F32))
    sel = En(nc.sbuf_tensor("sel", [128, 16], F32))
    selc = En(nc.sbuf_tensor("selc", [128, 16], F32))
    mask_s = En(nc.sbuf_tensor("mask_s", [128, 16], F32))
    sm = En(nc.sbuf_tensor("sm", [128, NT], F32))
    svar = En(nc.sbuf_tensor("svar", [128, NT], F32))
    sr1 = En(nc.sbuf_tensor("sr1", [128, NT], F32))
    tiny = En(nc.sbuf_tensor("tiny", [128, 8], F32))
    ep_tiny = En(nc.sbuf_tensor("ep_tiny", [128, 8], F32))
    hb_f32 = hblk[:].bitcast(F32)
    xc_al = hb_f32[:, 0:D]
    sq_al = hb_f32[:, D:2 * D]

    P_hg = [En(nc.psum_tensor(f"P_hg{i}", [128, 512], F32)) for i in range(2)]
    P_hu = [En(nc.psum_tensor(f"P_hu{i}", [128, 512], F32)) for i in range(2)]
    P_dn = [En(nc.psum_tensor(f"P_dn{i}", [128, 512], F32)) for i in range(2)]
    PTR = [En(nc.psum_tensor(f"PTR{i}", [128, 512], BF16)) for i in range(2)]

    sem_w = [p.sem(f"w{i}") for i in range(2)]
    sem_w2 = p.sem("w2")
    sem_wd = p.sem("wd")
    sem_xl = [p.sem(f"xl{i}") for i in range(2)]
    sem_xs = [p.sem(f"xs{i}") for i in range(2)]
    sem_T = p.sem("T")
    sem_V = p.sem("V")
    sem_S = p.sem("S")

    p.dma("gpsimd", gw_sb2[:].rearrange("p (c o) -> p c o", o=16),
          gwT.rearrange("(c p) o -> p c o", p=128), sem_w2)
    p.dma("gpsimd", eb_sb[:], ebias.broadcast_to([128, 16]), sem_w2)
    p.dma("gpsimd", id_sb[:], ident, sem_w2)
    const_ready = sem_w2.n

    C = {}
    for s in range(nseq):
        for t in range(NT):
            par = t % 2
            lwait = []
            if ("xtile_free", par) in C:
                lwait = [(sem_V, C[("xtile_free", par)])]
            p.dma("sync", xtile[par][:], xin[s, t * 128:(t + 1) * 128, :],
                  sem_xl[par], wait=lwait)
            xl_cnt = sem_xl[par].n

            def f_sum(v, xa=xtile[par][:], tt=t):
                v.reduce_sum(tiny[:, 0:1], xa, axis=AX.X)
                v.drain()
                v.tensor_scalar(sm[:, tt:tt + 1], tiny[:, 0:1], 1.0 / D,
                                None, ALU.mult)
                v.drain()
                return v.tensor_scalar(xc_al, xa, sm[:, tt:tt + 1], None,
                                       ALU.subtract)
            sw = [(sem_xl[par], xl_cnt)]
            if t == 0 and ("hblk_free",) in C:
                sw.append((sem_T, C[("hblk_free",)]))
            if t == 0:
                for pp in range(2):
                    if ("zst_free", pp) in C:
                        sw.append((sem_xs[pp], C[("zst_free", pp)]))
            p.emit("vector", f_sum, wait=sw, inc=[(sem_V, 1)], drain=True)
            vc = sem_V.n
            def f_sq(se, tt=t):
                return se.activation(sq_al, xc_al, AF.Square,
                                     accum_out=svar[:, tt:tt + 1])
            p.emit("scalar", f_sq, wait=[(sem_V, vc)], inc=[(sem_S, 1)],
                   drain=True)
            sc = sem_S.n
            def f_var(v, tt=t):
                v.tensor_scalar(tiny[:, 0:1], svar[:, tt:tt + 1], 1.0 / D,
                                1e-5, ALU.mult, ALU.add)
                v.drain()
                return v.reciprocal(tiny[:, 1:2], tiny[:, 0:1])
            p.emit("vector", f_var, wait=[(sem_S, sc)], inc=[(sem_V, 1)],
                   drain=True)
            vc2 = sem_V.n
            def f_rt(se, tt=t):
                return se.activation(sr1[:, tt:tt + 1], tiny[:, 1:2], AF.Sqrt)
            p.emit("scalar", f_rt, wait=[(sem_V, vc2)], inc=[(sem_S, 1)],
                   drain=True)
            sc2 = sem_S.n
            def f_apply(v, tt=t):
                return v.tensor_scalar(y1t_scr[:], xc_al, sr1[:, tt:tt + 1],
                                       None, ALU.mult)
            aw = [(sem_S, sc2)]
            if ("y1t_free",) in C:
                aw.append((sem_T, C[("y1t_free",)]))
            p.emit("vector", f_apply, wait=aw, inc=[(sem_V, 1)], drain=True)
            apply_done = sem_V.n
            for g in range(2):
                bk = g % 2
                tw = [(sem_V, apply_done), (sem_w2, const_ready)] if g == 0 else []
                if ("ptbank_free", bk) in C:
                    tw.append((sem_S, C[("ptbank_free", bk)]))
                for j in range(4):
                    c = g * 4 + j
                    def f_tr(cc=c, jj=j, bb=bk):
                        def f(te):
                            return nc.tensor.transpose(
                                PTR[bb][:, jj * 128:(jj + 1) * 128],
                                y1t_scr[:, cc * 128:(cc + 1) * 128], id_sb[:])
                        return f
                    p.emit("tensor", f_tr(), wait=tw if j == 0 else [],
                           inc=[(sem_T, 1)] if j == 3 else [])
                    tw = []
                trc = sem_T.n
                def f_ev(gg=g, bb=bk, tt=t):
                    def f(se):
                        dst3 = y1fm[:].rearrange("p (c t) -> p c t", t=T)[
                            :, gg * 4:(gg + 1) * 4, tt * 128:(tt + 1) * 128]
                        return se.copy(dst3, PTR[bb][:].rearrange(
                            "p (j t) -> p j t", t=128))
                    return f
                p.emit("scalar", f_ev(), wait=[(sem_T, trc)],
                       inc=[(sem_S, 1)], drain=True)
                C[("ptbank_free", bk)] = sem_S.n
            y1fm_tile_done = sem_S.n
            C[("y1t_free",)] = sem_T.n

            rw = [(sem_S, y1fm_tile_done)]
            if ("pdn0_sig",) in C:
                rw.append((sem_S, C[("pdn0_sig",)]))
            if ("pdn_free", 0) in C:
                rw.append((sem_V, C[("pdn_free", 0)]))
            for c in range(ND):
                def f_rmm(cc=c, tt=t):
                    def f(te):
                        return nc.tensor.matmul(
                            P_dn[0][:, 0:16],
                            y1fm[:, cc * T + tt * 128:cc * T + tt * 128 + 128],
                            gw_sb2[:, cc * 16:(cc + 1) * 16],
                            start=(cc == 0), stop=(cc == ND - 1))
                    return f
                p.emit("tensor", f_rmm(), wait=rw if c == 0 else [],
                       inc=[(sem_T, 1)] if c == ND - 1 else [])
                rw = []
            rmm_done = sem_T.n
            def f_sig(se, tt=t):
                return se.activation(scores[:, tt * 16:(tt + 1) * 16],
                                     P_dn[0][:, 0:16], AF.Sigmoid)
            p.emit("scalar", f_sig, wait=[(sem_T, rmm_done)],
                   inc=[(sem_S, 1)], drain=True)
            C[("pdn0_sig",)] = sem_S.n
            sig_done = sem_S.n
            def f_topk(v, tt=t):
                sco = scores[:, tt * 16:(tt + 1) * 16]
                v.tensor_tensor(sel[:], sco, eb_sb[:], ALU.add)
                v.drain()
                v.tensor_copy(selc[:], sel[:])
                v.drain()
                for it in range(7):
                    v.reduce_max(tiny[:, 2:3], sel[:], axis=AX.X)
                    v.drain()
                    v.tensor_scalar(mask_s[:], sel[:], tiny[:, 2:3], None,
                                    ALU.is_ge)
                    v.drain()
                    v.scalar_tensor_tensor(sel[:], mask_s[:], NEG_BIG, sel[:],
                                           ALU.mult, ALU.add)
                    v.drain()
                v.reduce_max(tiny[:, 2:3], sel[:], axis=AX.X)
                v.drain()
                v.tensor_scalar(mask_s[:], selc[:], tiny[:, 2:3], None,
                                ALU.is_ge)
                v.drain()
                v.tensor_tensor(mask_s[:], sco, mask_s[:], ALU.mult)
                v.drain()
                v.reduce_sum(tiny[:, 3:4], mask_s[:], axis=AX.X)
                v.drain()
                v.reciprocal(tiny[:, 4:5], tiny[:, 3:4])
                v.drain()
                return v.tensor_scalar(combine[:, tt * 16:(tt + 1) * 16],
                                       mask_s[:], tiny[:, 4:5], ROUTE_SCALE,
                                       ALU.mult, ALU.mult)
            p.emit("vector", f_topk, wait=[(sem_S, sig_done)],
                   inc=[(sem_V, 1)], drain=True)
            C[("xtile_free", par)] = sem_V.n
        y1_all = sem_S.n
        topk_all = sem_V.n

        def f_zero(v):
            return v.memset(yacc[:], 0.0)
        zw = [(sem_V, topk_all)]
        if ("yacc_free",) in C:
            zw.append((sem_xs[0], C[("yacc_free",)][0]))
            zw.append((sem_xs[1], C[("yacc_free",)][1]))
        p.emit("vector", f_zero, wait=zw, inc=[(sem_V, 1)], drain=True)
        yzero_done = sem_V.n

        hml_last = {}
        for e in range(NE):
            pe = e % 2
            ww = []
            if ("wgb_free", pe) in C:
                ww = [(sem_T, C[("wgb_free", pe)])]
            p.dma("gpsimd", wgb[pe][:].rearrange("p (c o) -> p c o", o=D),
                  wg_all[e].rearrange("(c p) o -> p c o", p=128),
                  sem_w[pe], wait=ww)
            p.dma("gpsimd", wub[pe][:].rearrange("p (c o) -> p c o", o=D),
                  wu_all[e].rearrange("(c p) o -> p c o", p=128), sem_w[pe])
            w_ready = sem_w[pe].n
            wdw = []
            if ("wdb_free",) in C:
                wdw = [(sem_T, C[("wdb_free",)])]
            p.dma("gpsimd", wdb[:].rearrange("p (c o) -> p c o", o=D),
                  wd_all[e].rearrange("(c p) o -> p c o", p=128),
                  sem_wd, wait=wdw)
            wd_ready = sem_wd.n

            first_mm = True
            for tb in range(4):
                for ic in range(ND):
                    bk = ic % 2
                    mw = []
                    if first_mm:
                        mw = [(sem_w[pe], w_ready), (sem_S, y1_all),
                              (sem_V, yzero_done)]
                        first_mm = False
                    if ("phg_free", bk) in C:
                        mw.append((sem_S, C[("phg_free", bk)]))
                    for c in range(ND):
                        def f_g(cc=c, icc=ic, bb=bk, tbb=tb, ee=pe):
                            def f(te):
                                return nc.tensor.matmul(
                                    P_hg[bb][:],
                                    wgb[ee][:, cc * D + icc * 128:
                                            cc * D + icc * 128 + 128],
                                    y1fm[:, cc * T + tbb * 512:
                                         cc * T + (tbb + 1) * 512],
                                    start=(cc == 0), stop=(cc == ND - 1))
                            return f
                        p.emit("tensor", f_g(), wait=mw if c == 0 else [],
                               inc=[(sem_T, 1)] if c == ND - 1 else [])
                        mw = []
                    hg_done = sem_T.n
                    mw2 = []
                    if ("phu_free", bk) in C:
                        mw2.append((sem_V, C[("phu_free", bk)]))
                    for c in range(ND):
                        def f_u(cc=c, icc=ic, bb=bk, tbb=tb, ee=pe):
                            def f(te):
                                return nc.tensor.matmul(
                                    P_hu[bb][:],
                                    wub[ee][:, cc * D + icc * 128:
                                            cc * D + icc * 128 + 128],
                                    y1fm[:, cc * T + tbb * 512:
                                         cc * T + (tbb + 1) * 512],
                                    start=(cc == 0), stop=(cc == ND - 1))
                            return f
                        p.emit("tensor", f_u(), wait=mw2 if c == 0 else [],
                               inc=[(sem_T, 1)] if c == ND - 1 else [])
                        mw2 = []
                    hu_done = sem_T.n
                    sw2 = [(sem_T, hg_done)]
                    if ("sil_free", bk) in C:
                        sw2.append((sem_V, C[("sil_free", bk)]))
                    def f_sil(bb=bk):
                        def f(se):
                            return se.activation(sil[bb][:], P_hg[bb][:],
                                                 AF.Silu)
                        return f
                    p.emit("scalar", f_sil(), wait=sw2, inc=[(sem_S, 1)],
                           drain=True)
                    C[("phg_free", bk)] = sem_S.n
                    sil_done = sem_S.n
                    hw = [(sem_S, sil_done), (sem_T, hu_done)]
                    key = ("dn_read", ic)
                    if key in C:
                        hw.append((sem_T, C[key]))
                    def f_h(bb=bk, icc=ic):
                        def f(v):
                            return v.tensor_tensor(
                                hblk[:, icc * 512:(icc + 1) * 512],
                                sil[bb][:], P_hu[bb][:], ALU.mult)
                        return f
                    p.emit("vector", f_h(), wait=hw, inc=[(sem_V, 1)],
                           drain=True)
                    C[("phu_free", bk)] = sem_V.n
                    C[("sil_free", bk)] = sem_V.n
                    hml_last[ic] = sem_V.n
                h_all = sem_V.n
                for tt in range(4):
                    for ob in range(2):
                        bk = (tt * 2 + ob) % 2
                        gt = tb * 4 + tt
                        dw = [(sem_V, h_all)]
                        if tt == 0 and ob == 0:
                            dw.append((sem_wd, wd_ready))
                        if e == 0 and tb == 0 and tt == 0 and ob == 0 \
                                and ("pdn0_sig",) in C:
                            dw.append((sem_S, C[("pdn0_sig",)]))
                        if ("pdn_free", bk) in C:
                            dw.append((sem_V, C[("pdn_free", bk)]))
                        for ic in range(ND):
                            def f_d(icc=ic, ttt=tt, oo=ob, bb=bk):
                                def f(te):
                                    return nc.tensor.matmul(
                                        P_dn[bb][:],
                                        hblk[:, icc * 512 + ttt * 128:
                                             icc * 512 + ttt * 128 + 128],
                                        wdb[:, icc * D + oo * 512:
                                            icc * D + oo * 512 + 512],
                                        start=(icc == 0), stop=(icc == ND - 1))
                                return f
                            p.emit("tensor", f_d(), wait=dw if ic == 0 else [],
                                   inc=[(sem_T, 1)] if ic == ND - 1 else [])
                            dw = []
                        dn_done = sem_T.n
                        def f_acc(gtt=gt, oo=ob, bb=bk, ee=e):
                            def f(v):
                                ysl = yacc[:, gtt * D + oo * 512:
                                           gtt * D + (oo + 1) * 512]
                                if ee == 16:
                                    return v.tensor_tensor(ysl, ysl,
                                                           P_dn[bb][:], ALU.add)
                                return v.scalar_tensor_tensor(
                                    ysl, P_dn[bb][:],
                                    combine[:, gtt * 16 + ee:gtt * 16 + ee + 1],
                                    ysl, ALU.mult, ALU.add)
                            return f
                        p.emit("vector", f_acc(), wait=[(sem_T, dn_done)],
                               inc=[(sem_V, 1)], drain=True)
                        C[("pdn_free", bk)] = sem_V.n
                for ic in range(ND):
                    C[("dn_read", ic)] = sem_T.n
            C[("wgb_free", pe)] = sem_T.n
            C[("wdb_free",)] = sem_T.n
        C[("hblk_free",)] = sem_T.n
        acc_all = sem_V.n

        yacc_stores = [0, 0]
        for t in range(NT):
            par = t % 2
            lw = [(sem_V, C.get(("xtile_free", par), 0))]
            p.dma("sync", xtile[par][:], xin[s, t * 128:(t + 1) * 128, :],
                  sem_xl[par], wait=[w for w in lw if w[1] > 0])
            xrl = sem_xl[par].n
            def f_e1(v, tt=t):
                ysl = yacc[:, tt * D:(tt + 1) * D]
                v.reduce_sum(ep_tiny[:, 0:1], ysl, axis=AX.X)
                v.drain()
                v.tensor_scalar(ep_tiny[:, 1:2], ep_tiny[:, 0:1], 1.0 / D,
                                None, ALU.mult)
                v.drain()
                return v.tensor_scalar(ysl, ysl, ep_tiny[:, 1:2], None,
                                       ALU.subtract)
            ew = [(sem_V, acc_all)] if t == 0 else []
            p.emit("vector", f_e1, wait=ew, inc=[(sem_V, 1)], drain=True)
            e1 = sem_V.n
            def f_e2(se, tt=t):
                return se.activation(sq_al, yacc[:, tt * D:(tt + 1) * D],
                                     AF.Square, accum_out=ep_tiny[:, 2:3])
            p.emit("scalar", f_e2, wait=[(sem_V, e1)], inc=[(sem_S, 1)],
                   drain=True)
            e2 = sem_S.n
            def f_e3(v):
                v.tensor_scalar(ep_tiny[:, 3:4], ep_tiny[:, 2:3], 1.0 / D,
                                1e-5, ALU.mult, ALU.add)
                v.drain()
                return v.reciprocal(ep_tiny[:, 4:5], ep_tiny[:, 3:4])
            p.emit("vector", f_e3, wait=[(sem_S, e2)], inc=[(sem_V, 1)],
                   drain=True)
            e3 = sem_V.n
            def f_e4(se):
                return se.activation(ep_tiny[:, 5:6], ep_tiny[:, 4:5], AF.Sqrt)
            p.emit("scalar", f_e4, wait=[(sem_V, e3)], inc=[(sem_S, 1)],
                   drain=True)
            e4 = sem_S.n
            def f_e5(par_=par, tt=t):
                def f(v):
                    ysl = yacc[:, tt * D:(tt + 1) * D]
                    v.tensor_scalar(ysl, ysl, ep_tiny[:, 5:6], None, ALU.mult)
                    v.drain()
                    return v.tensor_tensor(ysl, xtile[par_][:], ysl, ALU.add)
                return f
            p.emit("vector", f_e5(), wait=[(sem_S, e4), (sem_xl[par], xrl)],
                   inc=[(sem_V, 1)], drain=True)
            C[("xtile_free", par)] = sem_V.n
            outd = sem_V.n
            p.dma("sync", xout[s, t * 128:(t + 1) * 128, :],
                  yacc[:, t * D:(t + 1) * D], sem_xs[par],
                  wait=[(sem_V, outd)])
            yacc_stores[par] = sem_xs[par].n
            C[("zst_free", par)] = sem_xs[par].n
        C[("yacc_free",)] = (sem_xs[0].n, sem_xs[1].n)

    p.emit("sync", lambda h: None,
           wait=[(sem_xs[0], sem_xs[0].n), (sem_xs[1], sem_xs[1].n)])
    p.build()
    es.close()
    return nc


def _prep_l0ffn_feeds(inputs, xs):
    cst = host_consts()
    wg = np.concatenate([np.transpose(inputs["L0_ffn_w_gate"], (0, 2, 1)),
                         inputs["L0_ffn_shared_gate_w"].T[None]], 0)
    wu = np.concatenate([np.transpose(inputs["L0_ffn_w_up"], (0, 2, 1)),
                         inputs["L0_ffn_shared_up_w"].T[None]], 0)
    wd = np.concatenate([np.transpose(inputs["L0_ffn_w_down"], (0, 2, 1)),
                         inputs["L0_ffn_shared_down_w"].T[None]], 0)
    common = {
        "wg_all": np.ascontiguousarray(wg).astype(np.float32),
        "wu_all": np.ascontiguousarray(wu).astype(np.float32),
        "wd_all": np.ascontiguousarray(wd).astype(np.float32),
        "gwT": np.ascontiguousarray(inputs["L0_ffn_gate_w"].T),
        "ebias": np.ascontiguousarray(
            inputs["L0_ffn_expert_bias"].reshape(1, 16)),
        "ident": cst["ident"],
    }
    return [dict(common, xin=np.ascontiguousarray(x)) for x in xs]


def run_l0ffn(inputs, x, nseq=NSEQ):
    xs = [x[c::NCORE][:nseq] for c in range(NCORE)]
    nc = build_l0ffn(nseq)
    feeds = _prep_l0ffn_feeds(inputs, xs)
    res = run_bass_kernel_spmd(nc, feeds, core_ids=list(range(NCORE)))
    out = np.empty_like(x[:nseq * NCORE])
    for c in range(NCORE):
        out[c::NCORE] = res.results[c]["xout"]
    return out



def build_l1attn(nseq=NSEQ):
    nc = bass.Bass("TRN2", target_bir_lowering=False)
    xin = nc.dram_tensor("xin", [nseq, T, D], F32, kind="ExternalInput").ap()
    g_wq = nc.dram_tensor("g_wq", [D, D], F32, kind="ExternalInput").ap()
    g_kvd = nc.dram_tensor("g_kvd", [D, 256], F32, kind="ExternalInput").ap()
    g_kup = nc.dram_tensor("g_kup", [256, 256], F32, kind="ExternalInput").ap()
    g_vup = nc.dram_tensor("g_vup", [256, 256], F32, kind="ExternalInput").ap()
    g_wo = nc.dram_tensor("g_wo", [D, D], F32, kind="ExternalInput").ap()
    l_wq = nc.dram_tensor("l_wq", [D, D], F32, kind="ExternalInput").ap()
    l_wk = nc.dram_tensor("l_wk", [D, 256], F32, kind="ExternalInput").ap()
    l_wv = nc.dram_tensor("l_wv", [D, 256], F32, kind="ExternalInput").ap()
    l_wo = nc.dram_tensor("l_wo", [D, D], F32, kind="ExternalInput").ap()
    n1w = nc.dram_tensor("n1w", [1, D], F32, kind="ExternalInput").ap()
    n2w = nc.dram_tensor("n2w", [1, D], F32, kind="ExternalInput").ap()
    ident = nc.dram_tensor("ident", [128, 128], F32, kind="ExternalInput").ap()
    cos_l = nc.dram_tensor("cos_l", [T, 32], F32, kind="ExternalInput").ap()
    sin_l = nc.dram_tensor("sin_l", [T, 32], F32, kind="ExternalInput").ap()
    cos_g = nc.dram_tensor("cos_g", [T, 32], F32, kind="ExternalInput").ap()
    sin_g = nc.dram_tensor("sin_g", [T, 32], F32, kind="ExternalInput").ap()
    tri_i = nc.dram_tensor("tri_i", [128, 128], F32, kind="ExternalInput").ap()
    tri_e = nc.dram_tensor("tri_e", [128, 128], F32, kind="ExternalInput").ap()
    xout = nc.dram_tensor("xout", [nseq, T, D], F32, kind="ExternalOutput").ap()

    es = ExitStack()
    En = es.enter_context
    p = Prog(nc, es)

    ybf = En(nc.sbuf_tensor("ybf", [128, NT * D], BF16))
    qhm = En(nc.sbuf_tensor("qhm", [128, ND * T], BF16))
    khm = En(nc.sbuf_tensor("khm", [128, 2 * T], BF16))
    vtm = En(nc.sbuf_tensor("vtm", [128, NT * 256], BF16))
    latfm = En(nc.sbuf_tensor("latfm", [128, 2 * T], BF16))
    wbig = En(nc.sbuf_tensor("wbig", [128, ND * D], BF16))
    wkvd = En(nc.sbuf_tensor("wkvd", [128, ND * 256], BF16))
    wkup = En(nc.sbuf_tensor("wkup", [128, 2 * 256], BF16))
    wvup = En(nc.sbuf_tensor("wvup", [128, 2 * 256], BF16))
    wlk = En(nc.sbuf_tensor("wlk", [128, ND * 256], BF16))
    wlv = En(nc.sbuf_tensor("wlv", [128, ND * 256], BF16))
    n1_sb = En(nc.sbuf_tensor("n1_sb", [128, D], F32))
    n2_sb = En(nc.sbuf_tensor("n2_sb", [128, D], F32))
    id_sb = En(nc.sbuf_tensor("id_sb", [128, 128], BF16))
    cosl_sb = En(nc.sbuf_tensor("cosl_sb", [128, NT * 32], F32))
    sinl_sb = En(nc.sbuf_tensor("sinl_sb", [128, NT * 32], F32))
    cosg_sb = En(nc.sbuf_tensor("cosg_sb", [128, NT * 32], F32))
    sing_sb = En(nc.sbuf_tensor("sing_sb", [128, NT * 32], F32))
    tri_i_sb = En(nc.sbuf_tensor("tri_i_sb", [128, 128], F32))
    tri_e_sb = En(nc.sbuf_tensor("tri_e_sb", [128, 128], F32))
    xtile = [En(nc.sbuf_tensor(f"xt{i}", [128, D], F32)) for i in range(2)]
    xc_scr = En(nc.sbuf_tensor("xc_scr", [128, D], F32))
    sq_scr = En(nc.sbuf_tensor("sq_scr", [128, D], F32))
    xn_scr = En(nc.sbuf_tensor("xn_scr", [128, D], BF16))
    xnt_scr = En(nc.sbuf_tensor("xnt_scr", [128, ND * 128], BF16))
    qk_f32 = En(nc.sbuf_tensor("qk_f32", [128, D], F32))
    qrot_f32 = En(nc.sbuf_tensor("qrot_f32", [128, D], F32))
    qrot_bf = En(nc.sbuf_tensor("qrot_bf", [128, D], BF16))
    k_f32 = En(nc.sbuf_tensor("k_f32", [128, 256], F32))
    hn16 = En(nc.sbuf_tensor("hn16", [128, 16], F32))
    sm = En(nc.sbuf_tensor("sm", [128, NT], F32))
    svar = En(nc.sbuf_tensor("svar", [128, NT], F32))
    srr = En(nc.sbuf_tensor("srr", [128, NT], F32))
    sr1 = En(nc.sbuf_tensor("sr1", [128, NT], F32))
    tiny = En(nc.sbuf_tensor("tiny", [128, 8], F32))
    ep_tiny = En(nc.sbuf_tensor("ep_tiny", [128, 8], F32))
    P_sb = En(nc.sbuf_tensor("P_sb", [128, T], BF16))
    PT_sb = En(nc.sbuf_tensor("PT_sb", [128, T], BF16))
    rmax4 = En(nc.sbuf_tensor("rmax4", [128, 4], F32))
    nmax = En(nc.sbuf_tensor("nmax", [128, 1], F32))
    ssum4 = En(nc.sbuf_tensor("ssum4", [128, 4], F32))
    rsum = En(nc.sbuf_tensor("rsum", [128, 1], F32))
    afm = En(nc.sbuf_tensor("afm", [128, ND * 128], BF16))
    z_scr = [En(nc.sbuf_tensor(f"z{i}", [128, D], F32)) for i in range(2)]
    yf_scr = En(nc.sbuf_tensor("yf_scr", [128, D], F32))

    PA = En(nc.psum_tensor("PA", [128, T], F32))
    PB = En(nc.psum_tensor("PB", [128, D], F32))
    PT2 = [En(nc.psum_tensor(f"PT2{i}", [128, 512], BF16)) for i in range(2)]

    sem_w = p.sem("w")
    sem_xl = [p.sem(f"xl{i}") for i in range(2)]
    sem_xs = [p.sem(f"xs{i}") for i in range(2)]
    sem_T = p.sem("T")
    sem_V = p.sem("V")
    sem_S = p.sem("S")

    for (dst, srcd, width) in ((wkvd, g_kvd, 256), (wlk, l_wk, 256),
                               (wlv, l_wv, 256)):
        p.dma("gpsimd", dst[:].rearrange("p (c o) -> p c o", o=width),
              srcd.rearrange("(c p) o -> p c o", p=128), sem_w)
    p.dma("gpsimd", wkup[:].rearrange("p (c o) -> p c o", o=256),
          g_kup.rearrange("(c p) o -> p c o", p=128), sem_w)
    p.dma("gpsimd", wvup[:].rearrange("p (c o) -> p c o", o=256),
          g_vup.rearrange("(c p) o -> p c o", p=128), sem_w)
    p.dma("gpsimd", n1_sb[:], n1w.broadcast_to([128, D]), sem_w)
    p.dma("gpsimd", n2_sb[:], n2w.broadcast_to([128, D]), sem_w)
    p.dma("gpsimd", id_sb[:], ident, sem_w)
    for dst, srcd in ((cosl_sb, cos_l), (sinl_sb, sin_l),
                      (cosg_sb, cos_g), (sing_sb, sin_g)):
        p.dma("gpsimd", dst[:].rearrange("p (n f) -> p n f", f=32),
              srcd.rearrange("(n p) f -> p n f", p=128), sem_w)
    p.dma("gpsimd", tri_i_sb[:], tri_i, sem_w)
    p.dma("gpsimd", tri_e_sb[:], tri_e, sem_w)
    const_ready = sem_w.n

    C = {}

    def rope_qknorm(dst_bf, tt, nh, cos_b, sin_b, src_f32=None):
        if src_f32 is None:
            src_f32 = qk_f32
        def f_rope(v):
            base = src_f32[:, 0:nh * 64].rearrange(
                "p (h f two) -> p h f two", two=2, f=32)
            e = base[:, :, :, 0:1].squeeze(3)
            o = base[:, :, :, 1:2].squeeze(3)
            cosv = cos_b[:, tt * 32:(tt + 1) * 32].unsqueeze(1) \
                .broadcast_to([128, nh, 32])
            sinv = sin_b[:, tt * 32:(tt + 1) * 32].unsqueeze(1) \
                .broadcast_to([128, nh, 32])
            n = nh * 32
            t1 = xc_scr[:, 0:n].rearrange("p (h f) -> p h f", f=32)
            t2 = sq_scr[:, 0:n].rearrange("p (h f) -> p h f", f=32)
            t3 = xc_scr[:, n:2 * n].rearrange("p (h f) -> p h f", f=32)
            t4 = sq_scr[:, n:2 * n].rearrange("p (h f) -> p h f", f=32)
            dbase = qrot_f32[:, 0:nh * 64].rearrange(
                "p (h f two) -> p h f two", two=2, f=32)
            de = dbase[:, :, :, 0:1].squeeze(3)
            do = dbase[:, :, :, 1:2].squeeze(3)
            v.tensor_tensor(t1, e, cosv, ALU.mult)
            v.tensor_tensor(t2, o, sinv, ALU.mult)
            v.drain()
            v.tensor_tensor(de, t1, t2, ALU.subtract)
            v.tensor_tensor(t3, e, sinv, ALU.mult)
            v.tensor_tensor(t4, o, cosv, ALU.mult)
            v.drain()
            return v.tensor_tensor(do, t3, t4, ALU.add)
        return f_rope

    def emit_qknorm(nh, dst_bf, wait_rope_done, extra_mul_wait=()):
        def f_sq2(se):
            return se.activation(sq_scr[:, 0:nh * 64], qrot_f32[:, 0:nh * 64],
                                 AF.Square)
        p.emit("scalar", f_sq2, wait=wait_rope_done, inc=[(sem_S, 1)],
               drain=True)
        s1 = sem_S.n
        def f_red(v):
            v.tensor_reduce(hn16[:, 0:nh],
                            sq_scr[:, 0:nh * 64].rearrange(
                                "p (h d) -> p h d", d=64),
                            axis=AX.X, op=ALU.add)
            v.drain()
            return v.tensor_scalar(hn16[:, 0:nh], hn16[:, 0:nh], 1.0 / 64,
                                   1e-6, ALU.mult, ALU.add)
        p.emit("vector", f_red, wait=[(sem_S, s1)], inc=[(sem_V, 1)],
               drain=True)
        v1 = sem_V.n
        def f_sqrt(se):
            return se.activation(hn16[:, 0:nh], hn16[:, 0:nh], AF.Sqrt)
        p.emit("scalar", f_sqrt, wait=[(sem_V, v1)], inc=[(sem_S, 1)],
               drain=True)
        s2 = sem_S.n
        def f_mul(v):
            v.reciprocal(tiny[:, 0:1], hn16[:, 0:1])
            return v.nop()
        def f_mul2(v):
            v.reciprocal(hn16[:, 0:nh], hn16[:, 0:nh])
            v.drain()
            return v.tensor_tensor(
                dst_bf[:, 0:nh * 64].rearrange("p (h d) -> p h d", d=64),
                qrot_f32[:, 0:nh * 64].rearrange("p (h d) -> p h d", d=64),
                hn16[:, 0:nh].unsqueeze(2).broadcast_to([128, nh, 64]),
                ALU.mult)
        p.emit("vector", f_mul2, wait=[(sem_S, s2)] + list(extra_mul_wait),
               inc=[(sem_V, 1)], drain=True)
        return sem_V.n

    def transpose_to(dstb, src_bf, tt, nch, dst_T, extra_wait=()):
        for g in range((nch + 3) // 4):
            bk = g % 2
            tw = list(extra_wait) if g == 0 else []
            if ("ptbank_free", bk) in C:
                tw.append((sem_S, C[("ptbank_free", bk)]))
            jmax = min(4, nch - g * 4)
            for j in range(jmax):
                c = g * 4 + j
                def f_tr(cc=c, jj=j, bb=bk):
                    def f(te):
                        return nc.tensor.transpose(
                            PT2[bb][:, jj * 128:(jj + 1) * 128],
                            src_bf[:, cc * 128:(cc + 1) * 128], id_sb[:])
                    return f
                p.emit("tensor", f_tr(), wait=tw if j == 0 else [],
                       inc=[(sem_T, 1)] if j == jmax - 1 else [])
                tw = []
            trc = sem_T.n
            def f_ev(gg=g, bb=bk, tt_=tt, jm=jmax):
                def f(se):
                    dst3 = dstb[:].rearrange("p (c t) -> p c t", t=dst_T)[
                        :, gg * 4:gg * 4 + jm, tt_ * 128:(tt_ + 1) * 128]
                    return se.copy(dst3, PT2[bb][:, 0:jm * 128].rearrange(
                        "p (j t) -> p j t", t=128))
                return f
            p.emit("scalar", f_ev(), wait=[(sem_T, trc)],
                   inc=[(sem_S, 1)], drain=True)
            C[("ptbank_free", bk)] = sem_S.n
        return sem_S.n

    for s in range(nseq):
        ww = [(sem_T, C.get(("wbig_free",), 0)), (sem_w, sem_w.n)]
        p.dma("gpsimd", wbig[:].rearrange("p (c o) -> p c o", o=D),
              g_wq.rearrange("(c p) o -> p c o", p=128), sem_w,
              wait=[w for w in ww if w[1] > 0])
        wq_ready = sem_w.n

        for t in range(NT):
            par = t % 2
            lwait = []
            if ("xtile_free", par) in C:
                lwait = [(sem_V, C[("xtile_free", par)])]
            p.dma("sync", xtile[par][:], xin[s, t * 128:(t + 1) * 128, :],
                  sem_xl[par], wait=lwait)
            xl_cnt = sem_xl[par].n
            C["w_stats"] = [(sem_xl[par], xl_cnt), (sem_w, const_ready)]
            if t == 0:
                for pp in range(2):
                    if ("zst_free", pp) in C:
                        C["w_stats"].append((sem_xs[pp], C[("zst_free", pp)]))
            rr_ready = ln_stats_and_apply(
                p, nc, C, xtile[par][:], t, sm, svar, srr, sr1,
                xc_scr[:], sq_scr[:], tiny, sem_V, sem_S,
                double_ln=True, rms_second=True)
            def f_apply(v, tt=t):
                v.tensor_scalar(ybf[:, tt * D:(tt + 1) * D], xc_scr[:],
                                sr1[:, tt:tt + 1], None, ALU.mult)
                v.tensor_scalar(xc_scr[:], xc_scr[:], srr[:, tt:tt + 1],
                                None, ALU.mult)
                v.drain()
                return v.tensor_tensor(xn_scr[:], xc_scr[:], n1_sb[:],
                                       ALU.mult)
            aw = [(sem_V, rr_ready)]
            if ("xn_free",) in C:
                aw.append((sem_T, C[("xn_free",)]))
            if ("ybf_free",) in C and t == 0:
                aw.append((sem_S, C[("ybf_free",)]))
            p.emit("vector", f_apply, wait=aw, inc=[(sem_V, 1)], drain=True)
            C[("xtile_free", par)] = sem_V.n
            apply_done = sem_V.n
            xnt_ready = transpose_to(xnt_scr, xn_scr, 0, ND, 128,
                                     extra_wait=[(sem_V, apply_done)])

            qw = [(sem_S, xnt_ready), (sem_w, wq_ready)]
            if ("paq_free",) in C:
                qw.append((sem_S, C[("paq_free",)]))
            for ob in range(2):
                for c in range(ND):
                    def f_q(cc=c, oo=ob):
                        def f(te):
                            return nc.tensor.matmul(
                                PA[:, oo * 512:(oo + 1) * 512],
                                xnt_scr[:, cc * 128:(cc + 1) * 128],
                                wbig[:, cc * D + oo * 512:cc * D + oo * 512 + 512],
                                start=(cc == 0), stop=(cc == ND - 1))
                        return f
                    p.emit("tensor", f_q(), wait=qw if (c == 0 and ob == 0) else [],
                           inc=[(sem_T, 1)] if (c == ND - 1 and ob == 1) else [])
                    qw = []
            q_done = sem_T.n
            ew = [(sem_T, q_done)]
            if ("qkf_free",) in C:
                ew.append((sem_V, C[("qkf_free",)]))
            def f_qe(se):
                return se.copy(qk_f32[:], PA[:, 0:D])
            p.emit("scalar", f_qe, wait=ew, inc=[(sem_S, 1)], drain=True)
            C[("paq_free",)] = sem_S.n
            q_ev = sem_S.n
            lw2 = []
            if ("palat_free",) in C:
                lw2.append((sem_S, C[("palat_free",)]))
            for c in range(ND):
                def f_lat(cc=c):
                    def f(te):
                        return nc.tensor.matmul(
                            PA[:, 1024:1280],
                            xnt_scr[:, cc * 128:(cc + 1) * 128],
                            wkvd[:, cc * 256:(cc + 1) * 256],
                            start=(cc == 0), stop=(cc == ND - 1))
                    return f
                p.emit("tensor", f_lat(), wait=lw2 if c == 0 else [],
                       inc=[(sem_T, 1)] if c == ND - 1 else [])
                lw2 = []
            lat_done = sem_T.n
            lev = [(sem_T, lat_done)]
            def f_le(se):
                return se.copy(xn_scr[:, 0:256], PA[:, 1024:1280])
            p.emit("scalar", f_le, wait=lev, inc=[(sem_S, 1)], drain=True)
            C[("palat_free",)] = sem_S.n
            lat_ev = sem_S.n
            latfm_done = transpose_to(latfm, xn_scr, t, 2, T,
                                      extra_wait=[(sem_S, lat_ev)])
            C[("xn_free",)] = sem_T.n
            rw2 = [(sem_S, q_ev)]
            if ("qrot_free",) in C:
                rw2.append((sem_S, C[("qrot_free",)]))
            p.emit("vector", rope_qknorm(qrot_bf, t, 16, cosg_sb, sing_sb,
                                         None), wait=rw2,
                   inc=[(sem_V, 1)], drain=True)
            rope_done = sem_V.n
            C[("qkf_free",)] = sem_V.n
            qn_done = emit_qknorm(
                16, qrot_bf, [(sem_V, rope_done)],
                extra_mul_wait=[(sem_T, C[("qrot_free",)])]
                if ("qrot_free",) in C else ())
            transpose_to(qhm, qrot_bf, t, ND, T,
                         extra_wait=[(sem_V, qn_done)])
            C[("qrot_free",)] = sem_T.n

            kw = [(sem_S, latfm_done)]
            if ("pakv_free",) in C:
                kw.append((sem_S, C[("pakv_free",)]))
            for c in range(2):
                def f_k(cc=c, tt=t):
                    def f(te):
                        return nc.tensor.matmul(
                            PA[:, 1536:1792],
                            latfm[:, cc * T + tt * 128:cc * T + tt * 128 + 128],
                            wkup[:, cc * 256:(cc + 1) * 256],
                            start=(cc == 0), stop=(cc == 1))
                    return f
                p.emit("tensor", f_k(), wait=kw if c == 0 else [],
                       inc=[(sem_T, 1)] if c == 1 else [])
                kw = []
            for c in range(2):
                def f_v(cc=c, tt=t):
                    def f(te):
                        return nc.tensor.matmul(
                            PA[:, 1792:2048],
                            latfm[:, cc * T + tt * 128:cc * T + tt * 128 + 128],
                            wvup[:, cc * 256:(cc + 1) * 256],
                            start=(cc == 0), stop=(cc == 1))
                    return f
                p.emit("tensor", f_v(), inc=[(sem_T, 1)] if c == 1 else [])
            kv_done = sem_T.n
            kev = [(sem_T, kv_done)]
            if ("kf32_free",) in C:
                kev.append((sem_V, C[("kf32_free",)]))
            if ("vtm_free",) in C and t == 0:
                kev.append((sem_T, C[("vtm_free",)]))
            def f_kv_ev(tt=t):
                def f(se):
                    se.copy(k_f32[:], PA[:, 1536:1792])
                    return se.copy(vtm[:, tt * 256:(tt + 1) * 256],
                                   PA[:, 1792:2048])
                return f
            p.emit("scalar", f_kv_ev(), wait=kev, inc=[(sem_S, 1)], drain=True)
            C[("pakv_free",)] = sem_S.n
            k_ev = sem_S.n
            rw3 = [(sem_S, k_ev)]
            p.emit("vector", rope_qknorm(qrot_bf, t, 4, cosg_sb, sing_sb,
                                         k_f32), wait=rw3,
                   inc=[(sem_V, 1)], drain=True)
            ropek_done = sem_V.n
            C[("kf32_free",)] = sem_V.n
            qnk_done = emit_qknorm(
                4, qrot_bf, [(sem_V, ropek_done)],
                extra_mul_wait=[(sem_T, C[("qrot_free",)])])
            transpose_to(khm, qrot_bf, t, 2, T,
                         extra_wait=[(sem_V, qnk_done)])
            C[("qrot_free",)] = sem_T.n
        ga_all = sem_S.n

        p.dma("gpsimd", wbig[:].rearrange("p (c o) -> p c o", o=D),
              g_wo.rearrange("(c p) o -> p c o", p=128), sem_w,
              wait=[(sem_T, sem_T.n), (sem_w, sem_w.n)])
        wo_ready = sem_w.n

        def attention_phase(kind, wo_ready_cnt, last_phase):
            for qt in range(NT):
                for h in range(16):
                    kv = HEAD_AT_SLOT[h] // 4
                    kch, khf = kv // 2, (kv % 2) * 64
                    qch, qhf = h // 2, (h % 2) * 64
                    assert khf == qhf
                    if kind == "g":
                        kts = list(range(0, qt + 1))
                    else:
                        kts = [k for k in (qt - 2, qt - 1, qt) if k >= 0]
                    sw = [(sem_S, C[("phase_in",)])] if (qt == 0 and h == 0) else []
                    if ("s_free",) in C:
                        sw.append((sem_S, C[("s_free",)]))
                    for ki, kt in enumerate(kts):
                        def f_s(kt_=kt, ki_=ki, qch_=qch, qhf_=qhf,
                                kch_=kch, khf_=khf, qt_=qt):
                            def f(te):
                                return nc.tensor.matmul(
                                    PA[:, ki_ * 128:(ki_ + 1) * 128],
                                    qhm[qhf_:qhf_ + 64,
                                        qch_ * T + qt_ * 128:qch_ * T + qt_ * 128 + 128],
                                    khm[khf_:khf_ + 64,
                                        kch_ * T + kt_ * 128:kch_ * T + kt_ * 128 + 128],
                                    start=True, stop=True)
                            return f
                        p.emit("tensor", f_s(), wait=sw if ki == 0 else [],
                               inc=[(sem_T, 1)] if ki == len(kts) - 1 else [])
                        sw = []
                    s_done = sem_T.n
                    W = len(kts) * 128
                    nch = (W + 511) // 512
                    def f_mask(v, kts_=tuple(kts), qt_=qt, kind_=kind):
                        inst = None
                        for ki_, kt_ in enumerate(kts_):
                            if kt_ == qt_:
                                inst = v.tensor_tensor(
                                    PA[:, ki_ * 128:(ki_ + 1) * 128],
                                    PA[:, ki_ * 128:(ki_ + 1) * 128],
                                    tri_i_sb[:], ALU.add)
                            elif kind_ == "l" and kt_ == qt_ - 2:
                                inst = v.tensor_tensor(
                                    PA[:, ki_ * 128:(ki_ + 1) * 128],
                                    PA[:, ki_ * 128:(ki_ + 1) * 128],
                                    tri_e_sb[:], ALU.add)
                        if inst is None:
                            inst = v.nop()
                        return inst
                    mw = [(sem_T, s_done)]
                    if ("nmax_free",) in C:
                        mw.append((sem_S, C[("nmax_free",)]))
                    p.emit("vector", f_mask, wait=mw, inc=[(sem_V, 1)],
                           drain=True)
                    mask_done = sem_V.n
                    def f_max(v, nch_=nch, W_=W):
                        for kb in range(nch_):
                            w0 = kb * 512
                            w1 = min(W_, w0 + 512)
                            v.reduce_max(rmax4[:, kb:kb + 1],
                                         PA[:, w0:w1], axis=AX.X)
                        v.drain()
                        v.reduce_max(nmax[:, 0:1], rmax4[:, 0:nch_], axis=AX.X)
                        v.drain()
                        return v.tensor_scalar(nmax[:, 0:1], nmax[:, 0:1],
                                               -0.125, None, ALU.mult)
                    p.emit("vector", f_max, wait=[(sem_V, mask_done)],
                           inc=[(sem_V, 1)], drain=True)
                    max_done = sem_V.n
                    def f_exp(s_, nch_=nch, W_=W):
                        inst = None
                        for kb in range(nch_):
                            w0 = kb * 512
                            w1 = min(W_, w0 + 512)
                            inst = s_.activation(
                                P_sb[:, w0:w1], PA[:, w0:w1], AF.Exp,
                                scale=0.125, bias=nmax[:, 0:1],
                                accum_out=ssum4[:, kb:kb + 1])
                        return inst
                    ew = [(sem_V, max_done)]
                    if ("p_free",) in C:
                        ew.append((sem_T, C[("p_free",)]))
                    p.emit("scalar", f_exp, wait=ew, inc=[(sem_S, 1)],
                           drain=True)
                    C[("s_free",)] = sem_S.n
                    C[("nmax_free",)] = sem_S.n
                    exp_done = sem_S.n
                    def f_norm(v, nch_=nch, W_=W):
                        v.reduce_sum(rsum[:, 0:1], ssum4[:, 0:nch_], axis=AX.X)
                        v.drain()
                        v.reciprocal(rsum[:, 0:1], rsum[:, 0:1])
                        v.drain()
                        return v.tensor_scalar(P_sb[:, 0:W_], P_sb[:, 0:W_],
                                               rsum[:, 0:1], None, ALU.mult)
                    p.emit("vector", f_norm, wait=[(sem_S, exp_done)],
                           inc=[(sem_V, 1)], drain=True)
                    norm_done = sem_V.n
                    nkt = len(kts)
                    for g in range((nkt + 3) // 4):
                        bk = g % 2
                        tw = [(sem_V, norm_done)] if g == 0 else []
                        if ("ptbank_free", bk) in C:
                            tw.append((sem_S, C[("ptbank_free", bk)]))
                        jmax = min(4, nkt - g * 4)
                        for j in range(jmax):
                            kc = g * 4 + j
                            def f_pt(kcc=kc, bb=bk, jj=j):
                                def f(te):
                                    return nc.tensor.transpose(
                                        PT2[bb][:, jj * 128:(jj + 1) * 128],
                                        P_sb[:, kcc * 128:(kcc + 1) * 128],
                                        id_sb[:])
                                return f
                            p.emit("tensor", f_pt(), wait=tw if j == 0 else [],
                                   inc=[(sem_T, 1)] if j == jmax - 1 else [])
                            tw = []
                        g_done = sem_T.n
                        def f_pte(gg=g, bb=bk, jm=jmax):
                            def f(se):
                                return se.copy(
                                    PT_sb[:, gg * 512:gg * 512 + jm * 128],
                                    PT2[bb][:, 0:jm * 128])
                            return f
                        p.emit("scalar", f_pte(), wait=[(sem_T, g_done)],
                               inc=[(sem_S, 1)], drain=True)
                        C[("ptbank_free", bk)] = sem_S.n
                    pt_done = sem_S.n
                    C[("p_free",)] = sem_T.n
                    aw2 = [(sem_S, pt_done)]
                    if ("po_free",) in C:
                        aw2.append((sem_S, C[("po_free",)]))
                    if h == 0 and ("pb_free",) in C:
                        aw2.append((sem_S, C[("pb_free",)]))
                    for ki, kt in enumerate(kts):
                        def f_av(kt_=kt, ki_=ki, kv_=kv, nk_=len(kts)):
                            def f(te):
                                return nc.tensor.matmul(
                                    PB[0:64, 0:128],
                                    vtm[:, kt_ * 256 + kv_ * 64:
                                        kt_ * 256 + kv_ * 64 + 64],
                                    PT_sb[:, ki_ * 128:(ki_ + 1) * 128],
                                    start=(ki_ == 0), stop=(ki_ == nk_ - 1))
                            return f
                        p.emit("tensor", f_av(), wait=aw2 if ki == 0 else [],
                               inc=[(sem_T, 1)] if ki == len(kts) - 1 else [])
                        aw2 = []
                    av_done = sem_T.n
                    def f_ave(hh=h):
                        ch2, hf2 = hh // 2, (hh % 2) * 64
                        def f(se):
                            return se.copy(
                                afm[hf2:hf2 + 64, ch2 * 128:(ch2 + 1) * 128],
                                PB[0:64, 0:128])
                        return f
                    avw = [(sem_T, av_done)]
                    if h == 0 and ("afm_free",) in C:
                        avw.append((sem_T, C[("afm_free",)]))
                    p.emit("scalar", f_ave(), wait=avw, inc=[(sem_S, 1)],
                           drain=True)
                    C[("po_free",)] = sem_S.n
                heads_done = sem_S.n
                zw = [(sem_S, heads_done), (sem_w, wo_ready_cnt)]
                if ("pb_free",) in C:
                    zw.append((sem_S, C[("pb_free",)]))
                for ob in range(2):
                    for c in range(ND):
                        def f_wo(cc=c, oo=ob):
                            def f(te):
                                return nc.tensor.matmul(
                                    PB[:, oo * 512:(oo + 1) * 512],
                                    afm[:, cc * 128:(cc + 1) * 128],
                                    wbig[:, cc * D + oo * 512:
                                         cc * D + oo * 512 + 512],
                                    start=(cc == 0), stop=(cc == ND - 1))
                            return f
                        p.emit("tensor", f_wo(), wait=zw,
                               inc=[(sem_T, 1)] if (ob == 1 and c == ND - 1)
                               else [])
                        zw = []
                wo_done = sem_T.n
                C[("afm_free",)] = sem_T.n
                C[("wbig_free",)] = sem_T.n

                par = qt % 2
                if not last_phase:
                    ew2 = [(sem_T, wo_done)]
                    if ("zscr_free", par) in C:
                        ew2.append((sem_V, C[("zscr_free", par)]))
                    def f_z(par_=par):
                        def f(se):
                            return se.copy(z_scr[par_][:], PB[:])
                        return f
                    p.emit("scalar", f_z(), wait=ew2, inc=[(sem_S, 1)],
                           drain=True)
                    C[("pb_free",)] = sem_S.n
                    z1 = sem_S.n
                    def f_upd(par_=par, qtt=qt):
                        def f(v):
                            ysl = ybf[:, qtt * D:(qtt + 1) * D]
                            v.tensor_copy(yf_scr[:], ysl)
                            v.drain()
                            v.tensor_tensor(yf_scr[:], yf_scr[:],
                                            z_scr[par_][:], ALU.add)
                            v.drain()
                            return v.tensor_copy(ysl, yf_scr[:])
                        return f
                    p.emit("vector", f_upd(), wait=[(sem_S, z1)],
                           inc=[(sem_V, 1)], drain=True)
                    C[("zscr_free", par)] = sem_V.n
                    C[("ybf_upd", qt)] = sem_V.n
                else:
                    lw3 = [(sem_V, C.get(("xtile_free", par), 0))]
                    p.dma("sync", xtile[par][:],
                          xin[s, qt * 128:(qt + 1) * 128, :],
                          sem_xl[par], wait=[w for w in lw3 if w[1] > 0])
                    xrl = sem_xl[par].n
                    ew2 = [(sem_T, wo_done)]
                    if ("zst_free", par) in C:
                        ew2.append((sem_xs[par], C[("zst_free", par)]))
                    def f_z(par_=par):
                        def f(se):
                            return se.copy(z_scr[par_][:], PB[:])
                        return f
                    p.emit("scalar", f_z(), wait=ew2, inc=[(sem_S, 1)],
                           drain=True)
                    C[("pb_free",)] = sem_S.n
                    z1 = sem_S.n
                    def f_b(par_=par, qtt=qt):
                        def f(v):
                            v.tensor_copy(yf_scr[:], ybf[:, qtt * D:(qtt + 1) * D])
                            v.drain()
                            v.tensor_tensor(z_scr[par_][:], z_scr[par_][:],
                                            yf_scr[:], ALU.add)
                            v.drain()
                            v.reduce_sum(ep_tiny[:, 0:1], z_scr[par_][:],
                                         axis=AX.X)
                            v.drain()
                            v.tensor_scalar(ep_tiny[:, 1:2], ep_tiny[:, 0:1],
                                            1.0 / D, None, ALU.mult)
                            v.drain()
                            return v.tensor_scalar(z_scr[par_][:],
                                                   z_scr[par_][:],
                                                   ep_tiny[:, 1:2], None,
                                                   ALU.subtract)
                        return f
                    p.emit("vector", f_b(), wait=[(sem_S, z1)],
                           inc=[(sem_V, 1)], drain=True)
                    b1 = sem_V.n
                    def f_sq3(se, par_=par):
                        return se.activation(yf_scr[:], z_scr[par_][:],
                                             AF.Square,
                                             accum_out=ep_tiny[:, 2:3])
                    p.emit("scalar", f_sq3, wait=[(sem_V, b1)],
                           inc=[(sem_S, 1)], drain=True)
                    b2 = sem_S.n
                    def f_r(v):
                        v.tensor_scalar(ep_tiny[:, 3:4], ep_tiny[:, 2:3],
                                        1.0 / D, 1e-5, ALU.mult, ALU.add)
                        v.drain()
                        return v.reciprocal(ep_tiny[:, 4:5], ep_tiny[:, 3:4])
                    p.emit("vector", f_r, wait=[(sem_S, b2)],
                           inc=[(sem_V, 1)], drain=True)
                    b3 = sem_V.n
                    def f_s2(se):
                        return se.activation(ep_tiny[:, 5:6], ep_tiny[:, 4:5],
                                             AF.Sqrt)
                    p.emit("scalar", f_s2, wait=[(sem_V, b3)],
                           inc=[(sem_S, 1)], drain=True)
                    b4 = sem_S.n
                    def f_fin(par_=par):
                        def f(v):
                            v.tensor_scalar(z_scr[par_][:], z_scr[par_][:],
                                            ep_tiny[:, 5:6], None, ALU.mult)
                            v.drain()
                            return v.tensor_tensor(z_scr[par_][:],
                                                   xtile[par_][:],
                                                   z_scr[par_][:], ALU.add)
                        return f
                    p.emit("vector", f_fin(), wait=[(sem_S, b4),
                                                    (sem_xl[par], xrl)],
                           inc=[(sem_V, 1)], drain=True)
                    C[("xtile_free", par)] = sem_V.n
                    outd = sem_V.n
                    p.dma("sync", xout[s, qt * 128:(qt + 1) * 128, :],
                          z_scr[par][:], sem_xs[par], wait=[(sem_V, outd)])
                    C[("zst_free", par)] = sem_xs[par].n

        C[("phase_in",)] = ga_all
        attention_phase("g", wo_ready, last_phase=False)
        gb_all_V = sem_V.n

        p.dma("gpsimd", wbig[:].rearrange("p (c o) -> p c o", o=D),
              l_wq.rearrange("(c p) o -> p c o", p=128), sem_w,
              wait=[(sem_T, C[("wbig_free",)]), (sem_w, sem_w.n)])
        lwq_ready = sem_w.n
        for t in range(NT):
            def f_sq4(se, tt=t):
                return se.activation(sq_scr[:], ybf[:, tt * D:(tt + 1) * D],
                                     AF.Square, accum_out=tiny[:, 2:3])
            sw4 = [(sem_V, C[("ybf_upd", t)])]
            if ("sq4_free",) in C:
                sw4.append((sem_V, C[("sq4_free",)]))
            p.emit("scalar", f_sq4, wait=sw4, inc=[(sem_S, 1)], drain=True)
            s4 = sem_S.n
            def f_r2(v, tt=t):
                v.tensor_scalar(tiny[:, 3:4], tiny[:, 2:3], 1.0 / D, 1e-6,
                                ALU.mult, ALU.add)
                v.drain()
                return v.reciprocal(tiny[:, 4:5], tiny[:, 3:4])
            p.emit("vector", f_r2, wait=[(sem_S, s4)], inc=[(sem_V, 1)],
                   drain=True)
            C[("sq4_free",)] = sem_V.n
            v4 = sem_V.n
            def f_s4(se, tt=t):
                return se.activation(srr[:, tt:tt + 1], tiny[:, 4:5], AF.Sqrt)
            p.emit("scalar", f_s4, wait=[(sem_V, v4)], inc=[(sem_S, 1)],
                   drain=True)
            s5 = sem_S.n
            def f_ap2(v, tt=t):
                v.tensor_scalar(xc_scr[:], ybf[:, tt * D:(tt + 1) * D],
                                srr[:, tt:tt + 1], None, ALU.mult)
                v.drain()
                return v.tensor_tensor(xn_scr[:], xc_scr[:], n2_sb[:],
                                       ALU.mult)
            aw2 = [(sem_S, s5)]
            if ("xn_free",) in C:
                aw2.append((sem_T, C[("xn_free",)]))
            p.emit("vector", f_ap2, wait=aw2, inc=[(sem_V, 1)], drain=True)
            ap2_done = sem_V.n
            xnt_ready = transpose_to(xnt_scr, xn_scr, 0, ND, 128,
                                     extra_wait=[(sem_V, ap2_done)])
            C[("xn_free",)] = sem_T.n
            qw = [(sem_S, xnt_ready), (sem_w, lwq_ready)]
            if ("paq_free",) in C:
                qw.append((sem_S, C[("paq_free",)]))
            for ob in range(2):
                for c in range(ND):
                    def f_q(cc=c, oo=ob):
                        def f(te):
                            return nc.tensor.matmul(
                                PA[:, oo * 512:(oo + 1) * 512],
                                xnt_scr[:, cc * 128:(cc + 1) * 128],
                                wbig[:, cc * D + oo * 512:cc * D + oo * 512 + 512],
                                start=(cc == 0), stop=(cc == ND - 1))
                        return f
                    p.emit("tensor", f_q(), wait=qw if (c == 0 and ob == 0) else [],
                           inc=[(sem_T, 1)] if (c == ND - 1 and ob == 1) else [])
                    qw = []
            q_done = sem_T.n
            ew = [(sem_T, q_done)]
            if ("qkf_free",) in C:
                ew.append((sem_V, C[("qkf_free",)]))
            def f_qe2(se):
                return se.copy(qk_f32[:], PA[:, 0:D])
            p.emit("scalar", f_qe2, wait=ew, inc=[(sem_S, 1)], drain=True)
            C[("paq_free",)] = sem_S.n
            q_ev = sem_S.n
            kw = []
            if ("pakv_free",) in C:
                kw.append((sem_S, C[("pakv_free",)]))
            for c in range(ND):
                def f_k2(cc=c):
                    def f(te):
                        return nc.tensor.matmul(
                            PA[:, 1536:1792],
                            xnt_scr[:, cc * 128:(cc + 1) * 128],
                            wlk[:, cc * 256:(cc + 1) * 256],
                            start=(cc == 0), stop=(cc == ND - 1))
                    return f
                p.emit("tensor", f_k2(), wait=kw if c == 0 else [],
                       inc=[(sem_T, 1)] if c == ND - 1 else [])
                kw = []
            for c in range(ND):
                def f_v2(cc=c):
                    def f(te):
                        return nc.tensor.matmul(
                            PA[:, 1792:2048],
                            xnt_scr[:, cc * 128:(cc + 1) * 128],
                            wlv[:, cc * 256:(cc + 1) * 256],
                            start=(cc == 0), stop=(cc == ND - 1))
                    return f
                p.emit("tensor", f_v2(), inc=[(sem_T, 1)] if c == ND - 1 else [])
            kv_done = sem_T.n
            kev = [(sem_T, kv_done)]
            if ("kf32_free",) in C:
                kev.append((sem_V, C[("kf32_free",)]))
            if ("vtm_free2",) in C and t == 0:
                kev.append((sem_T, C[("vtm_free2",)]))
            def f_kv_ev2(tt=t):
                def f(se):
                    se.copy(k_f32[:], PA[:, 1536:1792])
                    return se.copy(vtm[:, tt * 256:(tt + 1) * 256],
                                   PA[:, 1792:2048])
                return f
            p.emit("scalar", f_kv_ev2(), wait=kev, inc=[(sem_S, 1)],
                   drain=True)
            C[("pakv_free",)] = sem_S.n
            kv_ev = sem_S.n
            rw2 = [(sem_S, q_ev)]
            if ("qrot_free",) in C:
                rw2.append((sem_T, C[("qrot_free",)]))
            p.emit("vector", rope_qknorm(qrot_bf, t, 16, cosl_sb, sinl_sb,
                                         None), wait=rw2,
                   inc=[(sem_V, 1)], drain=True)
            rope_done = sem_V.n
            C[("qkf_free",)] = sem_V.n
            qn_done = emit_qknorm(
                16, qrot_bf, [(sem_V, rope_done)],
                extra_mul_wait=[(sem_T, C[("qrot_free",)])])
            transpose_to(qhm, qrot_bf, t, ND, T,
                         extra_wait=[(sem_V, qn_done)])
            C[("qrot_free",)] = sem_T.n
            p.emit("vector", rope_qknorm(qrot_bf, t, 4, cosl_sb, sinl_sb,
                                         k_f32), wait=[(sem_S, kv_ev)],
                   inc=[(sem_V, 1)], drain=True)
            ropek_done = sem_V.n
            C[("kf32_free",)] = sem_V.n
            qnk_done = emit_qknorm(
                4, qrot_bf, [(sem_V, ropek_done)],
                extra_mul_wait=[(sem_T, C[("qrot_free",)])])
            transpose_to(khm, qrot_bf, t, 2, T,
                         extra_wait=[(sem_V, qnk_done)])
            C[("qrot_free",)] = sem_T.n
        la_all = sem_S.n

        p.dma("gpsimd", wbig[:].rearrange("p (c o) -> p c o", o=D),
              l_wo.rearrange("(c p) o -> p c o", p=128), sem_w,
              wait=[(sem_T, sem_T.n), (sem_w, sem_w.n)])
        lwo_ready = sem_w.n

        C[("phase_in",)] = la_all
        attention_phase("l", lwo_ready, last_phase=True)
        C[("vtm_free",)] = sem_T.n
        C[("vtm_free2",)] = sem_T.n
        C[("ybf_free",)] = sem_S.n

    p.emit("sync", lambda h: None,
           wait=[(sem_xs[0], sem_xs[0].n), (sem_xs[1], sem_xs[1].n)])
    p.build()
    es.close()
    return nc


def _prep_l1attn_feeds(inputs, xs):
    cst = host_consts()
    cg, sg = rope_tables(1e6)
    perm = np.concatenate([np.arange(h * 64, h * 64 + 64)
                           for h in HEAD_AT_SLOT])
    common = {
        "g_wq": np.ascontiguousarray(inputs["L1_attn_g_wq"][:, perm]),
        "g_kvd": np.ascontiguousarray(inputs["L1_attn_g_kv_down"]),
        "g_kup": np.ascontiguousarray(inputs["L1_attn_g_k_up"]),
        "g_vup": np.ascontiguousarray(inputs["L1_attn_g_v_up"]),
        "g_wo": np.ascontiguousarray(inputs["L1_attn_g_wo"][perm, :]),
        "l_wq": np.ascontiguousarray(inputs["L1_attn_l_wq"][:, perm]),
        "l_wk": np.ascontiguousarray(inputs["L1_attn_l_wk"]),
        "l_wv": np.ascontiguousarray(inputs["L1_attn_l_wv"]),
        "l_wo": np.ascontiguousarray(inputs["L1_attn_l_wo"][perm, :]),
        "n1w": np.ascontiguousarray(inputs["L1_attn_norm1_w"].reshape(1, D)),
        "n2w": np.ascontiguousarray(inputs["L1_attn_norm2_w"].reshape(1, D)),
        "ident": cst["ident"],
        "cos_l": cst["cos_l"], "sin_l": cst["sin_l"],
        "cos_g": cst["cos_g"], "sin_g": cst["sin_g"],
        "tri_i": cst["tri_incl"], "tri_e": cst["tri_excl"],
    }
    return [dict(common, xin=np.ascontiguousarray(x)) for x in xs]


def run_l1attn(inputs, x, nseq=NSEQ):
    xs = [x[c::NCORE][:nseq] for c in range(NCORE)]
    nc = build_l1attn(nseq)
    feeds = _prep_l1attn_feeds(inputs, xs)
    res = run_bass_kernel_spmd(nc, feeds, core_ids=list(range(NCORE)))
    out = np.empty_like(x[:nseq * NCORE])
    for c in range(NCORE):
        out[c::NCORE] = res.results[c]["xout"]
    return out



def build_l1ffn_a(nseq=NSEQ):
    nc = bass.Bass("TRN2", target_bir_lowering=False)
    xin = nc.dram_tensor("xin", [nseq, T, D], F32, kind="ExternalInput").ap()
    wqkv = nc.dram_tensor("wqkv", [D, 1536], F32, kind="ExternalInput").ap()
    l1w = nc.dram_tensor("l1w", [1, D], F32, kind="ExternalInput").ap()
    l1b = nc.dram_tensor("l1b", [1, D], F32, kind="ExternalInput").ap()
    ident = nc.dram_tensor("ident", [128, 128], F32, kind="ExternalInput").ap()
    q_out = nc.dram_tensor("q_out", [nseq, T, D], BF16, kind="ExternalOutput").ap()
    k_out = nc.dram_tensor("k_out", [nseq, T, 256], BF16, kind="ExternalOutput").ap()
    v_out = nc.dram_tensor("v_out", [nseq, T, 256], BF16, kind="ExternalOutput").ap()

    es = ExitStack()
    En = es.enter_context
    p = Prog(nc, es)

    wq_sb = En(nc.sbuf_tensor("wq_sb", [128, ND * 1536], BF16))
    l1w_sb = En(nc.sbuf_tensor("l1w_sb", [128, D], F32))
    l1b_sb = En(nc.sbuf_tensor("l1b_sb", [128, D], F32))
    id_sb = En(nc.sbuf_tensor("id_sb", [128, 128], BF16))
    xtile = [En(nc.sbuf_tensor(f"xt{i}", [128, D], F32)) for i in range(2)]
    xc_scr = En(nc.sbuf_tensor("xc_scr", [128, D], F32))
    sq_scr = En(nc.sbuf_tensor("sq_scr", [128, D], F32))
    xn_scr = En(nc.sbuf_tensor("xn_scr", [128, D], BF16))
    xnt_scr = En(nc.sbuf_tensor("xnt_scr", [128, ND * 128], BF16))
    qkv_bf = [En(nc.sbuf_tensor(f"qkv_bf{i}", [128, 1536], BF16))
              for i in range(2)]
    sm = En(nc.sbuf_tensor("sm", [128, NT], F32))
    svar = En(nc.sbuf_tensor("svar", [128, NT], F32))
    srr = En(nc.sbuf_tensor("srr", [128, NT], F32))
    sr1 = En(nc.sbuf_tensor("sr1", [128, NT], F32))
    tiny = En(nc.sbuf_tensor("tiny", [128, 8], F32))

    PA = En(nc.psum_tensor("PA", [128, 1024], F32))
    PTR = [En(nc.psum_tensor(f"PTR{i}", [128, 512], BF16)) for i in range(2)]

    sem_w = p.sem("w")
    sem_xl = [p.sem(f"xl{i}") for i in range(2)]
    sem_qs = [p.sem(f"qs{i}") for i in range(2)]
    sem_T = p.sem("T")
    sem_V = p.sem("V")
    sem_S = p.sem("S")

    p.dma("gpsimd", wq_sb[:].rearrange("p (c o) -> p c o", o=1536),
          wqkv.rearrange("(c p) o -> p c o", p=128), sem_w)
    p.dma("gpsimd", l1w_sb[:], l1w.broadcast_to([128, D]), sem_w)
    p.dma("gpsimd", l1b_sb[:], l1b.broadcast_to([128, D]), sem_w)
    p.dma("gpsimd", id_sb[:], ident, sem_w)
    w_ready = sem_w.n

    C = {}
    for s in range(nseq):
        for t in range(NT):
            par = t % 2
            lwait = []
            if ("xtile_free", par) in C:
                lwait = [(sem_V, C[("xtile_free", par)])]
            p.dma("sync", xtile[par][:], xin[s, t * 128:(t + 1) * 128, :],
                  sem_xl[par], wait=lwait)
            xl_cnt = sem_xl[par].n
            C["w_stats"] = [(sem_xl[par], xl_cnt), (sem_w, w_ready)]
            rr_ready = ln_stats_and_apply(
                p, nc, C, xtile[par][:], t, sm, svar, srr, sr1,
                xc_scr[:], sq_scr[:], tiny, sem_V, sem_S,
                double_ln=True, rms_second=False)
            def f_apply(v, tt=t):
                v.tensor_scalar(xc_scr[:], xc_scr[:], srr[:, tt:tt + 1],
                                None, ALU.mult)
                v.drain()
                v.tensor_tensor(xc_scr[:], xc_scr[:], l1w_sb[:], ALU.mult)
                v.drain()
                return v.tensor_tensor(xn_scr[:], xc_scr[:], l1b_sb[:], ALU.add)
            aw = [(sem_V, rr_ready)]
            if ("xn_free",) in C:
                aw.append((sem_T, C[("xn_free",)]))
            p.emit("vector", f_apply, wait=aw, inc=[(sem_V, 1)], drain=True)
            C[("xtile_free", par)] = sem_V.n
            apply_done = sem_V.n
            for g in range(2):
                bk = g % 2
                tw = [(sem_V, apply_done)] if g == 0 else []
                if ("ptbank_free", bk) in C:
                    tw.append((sem_S, C[("ptbank_free", bk)]))
                for j in range(4):
                    c = g * 4 + j
                    def f_tr(cc=c, jj=j, bb=bk):
                        def f(te):
                            return nc.tensor.transpose(
                                PTR[bb][:, jj * 128:(jj + 1) * 128],
                                xn_scr[:, cc * 128:(cc + 1) * 128], id_sb[:])
                        return f
                    p.emit("tensor", f_tr(), wait=tw if j == 0 else [],
                           inc=[(sem_T, 1)] if j == 3 else [])
                    tw = []
                trc = sem_T.n
                def f_ev(gg=g, bb=bk):
                    def f(se):
                        return se.copy(xnt_scr[:, gg * 512:(gg + 1) * 512],
                                       PTR[bb][:])
                    return f
                p.emit("scalar", f_ev(), wait=[(sem_T, trc)],
                       inc=[(sem_S, 1)], drain=True)
                C[("ptbank_free", bk)] = sem_S.n
            xnt_ready = sem_S.n
            C[("xn_free",)] = sem_T.n
            for ob in range(3):
                bank = ob % 2
                ps = PA[:, bank * 512:(bank + 1) * 512]
                mw = [(sem_S, xnt_ready)] if ob == 0 else []
                if ("pa_free", bank) in C:
                    mw.append((sem_S, C[("pa_free", bank)]))
                for c in range(ND):
                    def f_mm(cc=c, oo=ob, ps_=ps):
                        def f(te):
                            return nc.tensor.matmul(
                                ps_, xnt_scr[:, cc * 128:(cc + 1) * 128],
                                wq_sb[:, cc * 1536 + oo * 512:
                                      cc * 1536 + oo * 512 + 512],
                                start=(cc == 0), stop=(cc == ND - 1))
                        return f
                    p.emit("tensor", f_mm(),
                           wait=mw if c == 0 else [],
                           inc=[(sem_T, 1)] if c == ND - 1 else [])
                    mw = []
                mmc = sem_T.n
                def f_qe(oo=ob, ps_=ps, par_=par):
                    def f(se):
                        return se.copy(qkv_bf[par_][:, oo * 512:(oo + 1) * 512],
                                       ps_)
                    return f
                ew = [(sem_T, mmc)]
                if ob == 0 and ("qkv_free", par) in C:
                    ew.append((sem_qs[par], C[("qkv_free", par)]))
                p.emit("scalar", f_qe(), wait=ew, inc=[(sem_S, 1)], drain=True)
                C[("pa_free", bank)] = sem_S.n
            ev_done = sem_S.n
            p.dma("sync", q_out[s, t * 128:(t + 1) * 128, :],
                  qkv_bf[par][:, 0:D], sem_qs[par], wait=[(sem_S, ev_done)])
            p.dma("sync", k_out[s, t * 128:(t + 1) * 128, :],
                  qkv_bf[par][:, D:D + 256], sem_qs[par])
            p.dma("sync", v_out[s, t * 128:(t + 1) * 128, :],
                  qkv_bf[par][:, D + 256:D + 512], sem_qs[par])
            C[("qkv_free", par)] = sem_qs[par].n
    p.emit("sync", lambda h: None,
           wait=[(sem_qs[0], sem_qs[0].n), (sem_qs[1], sem_qs[1].n)])
    p.build()
    es.close()
    return nc



def build_l1ffn_b1(nseq=NSEQ):
    S32 = 32
    nc = bass.Bass("TRN2", target_bir_lowering=False)
    xin = nc.dram_tensor("xin", [nseq, T, D], F32, kind="ExternalInput").ap()
    q_in = nc.dram_tensor("q_in", [nseq, T, D], BF16, kind="ExternalInput").ap()
    k_all = nc.dram_tensor("k_all", [S32, T, 256], BF16, kind="ExternalInput").ap()
    v_all = nc.dram_tensor("v_all", [S32, T, 256], BF16, kind="ExternalInput").ap()
    projT = nc.dram_tensor("projT", [D, D], F32, kind="ExternalInput").ap()
    masks = nc.dram_tensor("masks", [nseq, 32], F32, kind="ExternalInput").ap()
    ident = nc.dram_tensor("ident", [128, 128], F32, kind="ExternalInput").ap()
    ham = nc.dram_tensor("ham", [nseq, T, D], BF16, kind="ExternalOutput").ap()

    es = ExitStack()
    En = es.enter_context
    p = Prog(nc, es)

    kv_sb = En(nc.sbuf_tensor("kv_sb", [128, S32 * 512], BF16))
    q_sb = En(nc.sbuf_tensor("q_sb", [128, NSEQ * D], BF16))
    wp_sb = En(nc.sbuf_tensor("wp_sb", [128, ND * D], BF16))
    mask_sb = En(nc.sbuf_tensor("mask_sb", [128, NSEQ * 32], F32))
    id_sb = En(nc.sbuf_tensor("id_sb", [128, 128], BF16))
    sco = En(nc.sbuf_tensor("sco", [128, S32 * 16], F32))
    prob_bf = En(nc.sbuf_tensor("prob_bf", [128, S32 * 16], BF16))
    tmp1 = En(nc.sbuf_tensor("tmp1", [128, D], F32))
    acc = En(nc.sbuf_tensor("acc", [128, D], F32))
    acc_bf = En(nc.sbuf_tensor("acc_bf", [128, D], BF16))
    accT = En(nc.sbuf_tensor("accT", [128, ND * 128], BF16))
    mx16 = En(nc.sbuf_tensor("mx16", [128, 16], F32))
    sum16 = En(nc.sbuf_tensor("sum16", [128, 16], F32))
    xtile = En(nc.sbuf_tensor("xt", [128, D], F32))
    ep_tiny = En(nc.sbuf_tensor("ep_tiny", [128, 8], F32))
    ham_bf = [En(nc.sbuf_tensor(f"ham_bf{i}", [128, D], BF16)) for i in range(2)]
    z_scr = En(nc.sbuf_tensor("z_scr", [128, D], F32))

    PQ = En(nc.psum_tensor("PQ", [128, D], F32))
    PTR = [En(nc.psum_tensor(f"PTR{i}", [128, 512], BF16)) for i in range(2)]

    sem_w = p.sem("w")
    sem_kv = p.sem("kv")
    sem_xl = p.sem("xl")
    sem_hs = [p.sem(f"hs{i}") for i in range(2)]
    sem_T = p.sem("T")
    sem_V = p.sem("V")
    sem_S = p.sem("S")
    sem_G = p.sem("G")

    p.dma("gpsimd", wp_sb[:].rearrange("p (c o) -> p c o", o=D),
          projT.rearrange("(c p) o -> p c o", p=128), sem_w)
    p.dma("gpsimd", mask_sb[:].rearrange("p (s k) -> p s k", k=32),
          masks.unsqueeze(0).broadcast_to([128, nseq, 32]), sem_w)
    p.dma("gpsimd", id_sb[:], ident, sem_w)
    w_ready = sem_w.n

    C = {}
    for bt in range(NT):
        kvw = [(sem_G, C.get(("kv_free",), 0))]
        p.dma("sync", kv_sb[:].rearrange("p (s x) -> p s x", x=512)[:, :, 0:256],
              k_all[:, bt * 128:(bt + 1) * 128, :].rearrange("s p d -> p s d"),
              sem_kv, wait=[w for w in kvw if w[1] > 0])
        p.dma("sync", kv_sb[:].rearrange("p (s x) -> p s x", x=512)[:, :, 256:512],
              v_all[:, bt * 128:(bt + 1) * 128, :].rearrange("s p d -> p s d"),
              sem_kv)
        qw = [(sem_V, C.get(("q_free",), 0))]
        p.dma("sync", q_sb[:].rearrange("p (s d) -> p s d", d=D),
              q_in[:, bt * 128:(bt + 1) * 128, :].rearrange("s p d -> p s d"),
              sem_kv, wait=[w for w in qw if w[1] > 0])
        kv_ready = sem_kv.n

        for sl in range(nseq):
            for sp in range(S32):
                def f_qk(sl_=sl, sp_=sp):
                    def f(v):
                        qv = q_sb[:, sl_ * D:(sl_ + 1) * D].rearrange(
                            "p (kv r d) -> p kv r d", kv=4, r=4)
                        kvv = kv_sb[:, sp_ * 512:sp_ * 512 + 256].rearrange(
                            "p (kv d) -> p kv d", kv=4).unsqueeze(2) \
                            .broadcast_to([128, 4, 4, 64])
                        v.tensor_tensor(
                            tmp1[:].rearrange("p (kv r d) -> p kv r d",
                                              kv=4, r=4), qv, kvv, ALU.mult)
                        v.drain()
                        return v.tensor_reduce(
                            sco[:, sp_ * 16:(sp_ + 1) * 16],
                            tmp1[:].rearrange("p (h d) -> p h d", d=64),
                            axis=AX.X, op=ALU.add)
                    return f
                ws = []
                if sp == 0:
                    if sl == 0:
                        ws = [(sem_kv, kv_ready), (sem_w, w_ready)]
                    else:
                        ws = [(sem_kv, kv_ready)]
                    if ("sco_free",) in C:
                        ws.append((sem_V, C[("sco_free",)]))
                    if ("tmp1_g",) in C:
                        ws.append((sem_G, C[("tmp1_g",)]))
                    if ("tmp1_s",) in C:
                        ws.append((sem_S, C[("tmp1_s",)]))
                p.emit("vector", f_qk(), wait=ws,
                       inc=[(sem_V, 1)] if sp == S32 - 1 else [], drain=True)
            qk_done = sem_V.n
            def f_sm1(v, sl_=sl):
                sv = sco[:].rearrange("p (s h) -> p s h", h=16)
                mv = mask_sb[:, sl_ * 32:(sl_ + 1) * 32].unsqueeze(2) \
                    .broadcast_to([128, 32, 16])
                v.tensor_tensor(sv, sv, mv, ALU.add)
                v.drain()
                v.tensor_reduce(mx16[:], sco[:].rearrange(
                    "p (s h) -> p h s", h=16), axis=AX.X, op=ALU.max)
                v.drain()
                return v.tensor_tensor(
                    sv, sv, mx16[:].unsqueeze(1).broadcast_to([128, 32, 16]),
                    ALU.subtract)
            p.emit("vector", f_sm1, wait=[(sem_V, qk_done)],
                   inc=[(sem_V, 1)], drain=True)
            sm1 = sem_V.n
            def f_exp(se):
                return se.activation(tmp1[:, 0:512], sco[:], AF.Exp,
                                     scale=0.125)
            p.emit("scalar", f_exp, wait=[(sem_V, sm1)], inc=[(sem_S, 1)],
                   drain=True)
            expd = sem_S.n
            def f_sm2(v):
                v.tensor_reduce(sum16[:], tmp1[:, 0:512].rearrange(
                    "p (s h) -> p h s", h=16), axis=AX.X, op=ALU.add)
                v.drain()
                v.reciprocal(sum16[:], sum16[:])
                v.drain()
                return v.tensor_tensor(
                    prob_bf[:].rearrange("p (s h) -> p s h", h=16),
                    tmp1[:, 0:512].rearrange("p (s h) -> p s h", h=16),
                    sum16[:].unsqueeze(1).broadcast_to([128, 32, 16]),
                    ALU.mult)
            p.emit("vector", f_sm2, wait=[(sem_S, expd)],
                   inc=[(sem_V, 1)], drain=True)
            prob_done = sem_V.n
            C[("sco_free",)] = sem_V.n
            def f_avz(g):
                return g.memset(acc[:], 0.0)
            aw = [(sem_V, prob_done)]
            if ("acc_free",) in C:
                aw.append((sem_S, C[("acc_free",)]))
            p.emit("gpsimd", f_avz, wait=aw, inc=[(sem_G, 1)], drain=True)
            for sp in range(S32):
                def f_av(sp_=sp):
                    def f(g):
                        vv = kv_sb[:, sp_ * 512 + 256:sp_ * 512 + 512] \
                            .rearrange("p (kv d) -> p kv d", kv=4) \
                            .unsqueeze(2).broadcast_to([128, 4, 4, 64])
                        pv = prob_bf[:, sp_ * 16:(sp_ + 1) * 16] \
                            .unsqueeze(2).broadcast_to([128, 16, 64]) \
                            .rearrange("p (kv r) d -> p kv r d", kv=4)
                        g.tensor_tensor(
                            tmp1[:].rearrange("p (kv r d) -> p kv r d",
                                              kv=4, r=4), vv, pv, ALU.mult)
                        g.drain()
                        return g.tensor_tensor(acc[:], acc[:], tmp1[:], ALU.add)
                    return f
                p.emit("gpsimd", f_av(),
                       inc=[(sem_G, 1)] if sp == S32 - 1 else [], drain=True)
            av_done = sem_G.n
            C[("kv_free",)] = av_done
            C[("tmp1_g",)] = av_done
            def f_cast(v):
                return v.tensor_copy(acc_bf[:], acc[:])
            p.emit("vector", f_cast, wait=[(sem_G, av_done)],
                   inc=[(sem_V, 1)], drain=True)
            cast_done = sem_V.n
            for g in range(2):
                bk = g % 2
                tw = [(sem_V, cast_done)] if g == 0 else []
                if ("ptbank_free", bk) in C:
                    tw.append((sem_S, C[("ptbank_free", bk)]))
                for j in range(4):
                    c = g * 4 + j
                    def f_tr(cc=c, jj=j, bb=bk):
                        def f(te):
                            return nc.tensor.transpose(
                                PTR[bb][:, jj * 128:(jj + 1) * 128],
                                acc_bf[:, cc * 128:(cc + 1) * 128], id_sb[:])
                        return f
                    p.emit("tensor", f_tr(), wait=tw if j == 0 else [],
                           inc=[(sem_T, 1)] if j == 3 else [])
                    tw = []
                trc = sem_T.n
                def f_ev(gg=g, bb=bk):
                    def f(se):
                        return se.copy(accT[:, gg * 512:(gg + 1) * 512],
                                       PTR[bb][:])
                    return f
                p.emit("scalar", f_ev(), wait=[(sem_T, trc)],
                       inc=[(sem_S, 1)], drain=True)
                C[("ptbank_free", bk)] = sem_S.n
            accT_ready = sem_S.n
            C[("acc_free",)] = accT_ready
            zw = [(sem_S, accT_ready)]
            if ("pq_free",) in C:
                zw.append((sem_S, C[("pq_free",)]))
            for ob in range(2):
                for c in range(ND):
                    def f_wo(cc=c, oo=ob):
                        def f(te):
                            return nc.tensor.matmul(
                                PQ[:, oo * 512:(oo + 1) * 512],
                                accT[:, cc * 128:(cc + 1) * 128],
                                wp_sb[:, cc * D + oo * 512:cc * D + oo * 512 + 512],
                                start=(cc == 0), stop=(cc == ND - 1))
                        return f
                    p.emit("tensor", f_wo(), wait=zw,
                           inc=[(sem_T, 1)] if (ob == 1 and c == ND - 1) else [])
                    zw = []
            proj_done = sem_T.n
            par = sl % 2
            lw = [(sem_V, C.get(("xt_free",), 0))]
            p.dma("sync", xtile[:], xin[sl, bt * 128:(bt + 1) * 128, :],
                  sem_xl, wait=[w for w in lw if w[1] > 0])
            xrl = sem_xl.n
            def f_st(v):
                v.reduce_sum(ep_tiny[:, 0:1], xtile[:], axis=AX.X)
                v.drain()
                v.tensor_scalar(ep_tiny[:, 1:2], ep_tiny[:, 0:1], 1.0 / D,
                                None, ALU.mult)
                v.drain()
                return v.tensor_scalar(z_scr[:], xtile[:], ep_tiny[:, 1:2],
                                       None, ALU.subtract)
            sw2 = [(sem_xl, xrl)]
            if ("z_free", ) in C:
                sw2.append((sem_S, C[("z_free",)]))
            p.emit("vector", f_st, wait=sw2, inc=[(sem_V, 1)], drain=True)
            C[("xt_free",)] = sem_V.n
            st1 = sem_V.n
            def f_sq(se):
                return se.activation(tmp1[:], z_scr[:], AF.Square,
                                     accum_out=ep_tiny[:, 2:3])
            p.emit("scalar", f_sq, wait=[(sem_V, st1), (sem_G, av_done)],
                   inc=[(sem_S, 1)], drain=True)
            st2 = sem_S.n
            def f_r(v):
                v.tensor_scalar(ep_tiny[:, 3:4], ep_tiny[:, 2:3], 1.0 / D,
                                1e-5, ALU.mult, ALU.add)
                v.drain()
                return v.reciprocal(ep_tiny[:, 4:5], ep_tiny[:, 3:4])
            p.emit("vector", f_r, wait=[(sem_S, st2)], inc=[(sem_V, 1)],
                   drain=True)
            st3 = sem_V.n
            def f_s(se):
                return se.activation(ep_tiny[:, 5:6], ep_tiny[:, 4:5], AF.Sqrt)
            p.emit("scalar", f_s, wait=[(sem_V, st3)], inc=[(sem_S, 1)],
                   drain=True)
            st4 = sem_S.n
            def f_zp(se, par_=par):
                return se.copy(tmp1[:], PQ[:])
            zpw = [(sem_T, proj_done), (sem_S, st4)]
            p.emit("scalar", f_zp, wait=zpw, inc=[(sem_S, 1)], drain=True)
            C[("pq_free",)] = sem_S.n
            C[("z_free",)] = sem_S.n
            zp = sem_S.n
            def f_fin(par_=par):
                def f(v):
                    v.tensor_scalar(z_scr[:], z_scr[:], ep_tiny[:, 5:6],
                                    None, ALU.mult)
                    v.drain()
                    return v.tensor_tensor(ham_bf[par_][:], z_scr[:],
                                           tmp1[:], ALU.add)
                return f
            fw = [(sem_S, zp)]
            if ("ham_free", par) in C:
                fw.append((sem_hs[par], C[("ham_free", par)]))
            p.emit("vector", f_fin(), wait=fw, inc=[(sem_V, 1)], drain=True)
            C[("q_free",)] = sem_V.n
            find = sem_V.n
            p.dma("sync", ham[sl, bt * 128:(bt + 1) * 128, :],
                  ham_bf[par][:], sem_hs[par], wait=[(sem_V, find)])
            C[("ham_free", par)] = sem_hs[par].n
            C[("tmp1_s",)] = zp
    p.emit("sync", lambda h: None,
           wait=[(sem_hs[0], sem_hs[0].n), (sem_hs[1], sem_hs[1].n)])
    p.build()
    es.close()
    return nc



def build_l1ffn_b2(nseq=NSEQ):
    NE, F = 8, 2048
    NF = F // 128
    nc = bass.Bass("TRN2", target_bir_lowering=False)
    xin = nc.dram_tensor("xin", [nseq, T, D], F32, kind="ExternalInput").ap()
    ham = nc.dram_tensor("ham", [nseq, T, D], BF16, kind="ExternalInput").ap()
    w1_all = nc.dram_tensor("w1_all", [NE, D, F], F32, kind="ExternalInput").ap()
    w2_all = nc.dram_tensor("w2_all", [NE, F, D], F32, kind="ExternalInput").ap()
    routT = nc.dram_tensor("routT", [D, 8], F32, kind="ExternalInput").ap()
    l2w = nc.dram_tensor("l2w", [1, D], F32, kind="ExternalInput").ap()
    l2b = nc.dram_tensor("l2b", [1, D], F32, kind="ExternalInput").ap()
    ident = nc.dram_tensor("ident", [128, 128], F32, kind="ExternalInput").ap()
    xout = nc.dram_tensor("xout", [nseq, T, D], F32, kind="ExternalOutput").ap()

    es = ExitStack()
    En = es.enter_context
    p = Prog(nc, es)

    ln2fm = En(nc.sbuf_tensor("ln2fm", [128, ND * T], BF16))
    yacc = En(nc.sbuf_tensor("yacc", [128, NT * D], F32))
    w1_sb = En(nc.sbuf_tensor("w1_sb", [128, ND * F], BF16))
    w2_sb = En(nc.sbuf_tensor("w2_sb", [128, NF * D], BF16))
    hblk = En(nc.sbuf_tensor("hblk", [128, NF * 512], BF16))
    rt_sb = En(nc.sbuf_tensor("rt_sb", [128, ND * 8], BF16))
    l2w_sb = En(nc.sbuf_tensor("l2w_sb", [128, D], F32))
    l2b_sb = En(nc.sbuf_tensor("l2b_sb", [128, D], F32))
    id_sb = En(nc.sbuf_tensor("id_sb", [128, 128], BF16))
    wts = En(nc.sbuf_tensor("wts", [128, NT * 8], F32))
    gate8 = En(nc.sbuf_tensor("gate8", [128, 8], F32))
    sel8 = En(nc.sbuf_tensor("sel8", [128, 8], F32))
    mask8 = En(nc.sbuf_tensor("mask8", [128, 8], F32))
    hamt = [En(nc.sbuf_tensor(f"hamt{i}", [128, D], BF16)) for i in range(2)]
    xtile = [En(nc.sbuf_tensor(f"xt{i}", [128, D], F32)) for i in range(2)]
    xc_scr = En(nc.sbuf_tensor("xc_scr", [128, D], F32))
    sq_scr = En(nc.sbuf_tensor("sq_scr", [128, D], F32))
    xn_scr = En(nc.sbuf_tensor("xn_scr", [128, D], BF16))
    sm = En(nc.sbuf_tensor("sm", [128, NT], F32))
    sr1 = En(nc.sbuf_tensor("sr1", [128, NT], F32))
    tiny = En(nc.sbuf_tensor("tiny", [128, 8], F32))
    ep_tiny = En(nc.sbuf_tensor("ep_tiny", [128, 8], F32))

    P_h = [En(nc.psum_tensor(f"P_h{i}", [128, 512], F32)) for i in range(2)]
    P_dn = [En(nc.psum_tensor(f"P_dn{i}", [128, 512], F32)) for i in range(2)]
    PTR = [En(nc.psum_tensor(f"PTR{i}", [128, 512], BF16)) for i in range(2)]

    sem_w1 = p.sem("w1")
    sem_w2s = p.sem("w2s")
    sem_wc = p.sem("wc")
    sem_hl = [p.sem(f"hl{i}") for i in range(2)]
    sem_xl = [p.sem(f"xl{i}") for i in range(2)]
    sem_xs = [p.sem(f"xs{i}") for i in range(2)]
    sem_T = p.sem("T")
    sem_V = p.sem("V")
    sem_S = p.sem("S")

    p.dma("gpsimd", rt_sb[:].rearrange("p (c o) -> p c o", o=8),
          routT.rearrange("(c p) o -> p c o", p=128), sem_wc)
    p.dma("gpsimd", l2w_sb[:], l2w.broadcast_to([128, D]), sem_wc)
    p.dma("gpsimd", l2b_sb[:], l2b.broadcast_to([128, D]), sem_wc)
    p.dma("gpsimd", id_sb[:], ident, sem_wc)
    const_ready = sem_wc.n

    C = {}
    for s in range(nseq):
        for t in range(NT):
            par = t % 2
            hw = []
            if ("hamt_free", par) in C:
                hw = [(sem_V, C[("hamt_free", par)])]
            p.dma("sync", hamt[par][:], ham[s, t * 128:(t + 1) * 128, :],
                  sem_hl[par], wait=hw)
            hl_cnt = sem_hl[par].n
            def f_sum(v, ha=hamt[par][:], tt=t):
                v.reduce_sum(tiny[:, 0:1], ha, axis=AX.X)
                v.drain()
                v.tensor_scalar(sm[:, tt:tt + 1], tiny[:, 0:1], 1.0 / D,
                                None, ALU.mult)
                v.drain()
                return v.tensor_scalar(xc_scr[:], ha, sm[:, tt:tt + 1],
                                       None, ALU.subtract)
            sw = [(sem_hl[par], hl_cnt), (sem_wc, const_ready)]
            if t == 0 and ("moe_done",) in C:
                sw.append((sem_T, C[("moe_done",)]))
            if ("xc_free",) in C:
                sw.append((sem_S, C[("xc_free",)]))
            p.emit("vector", f_sum, wait=sw, inc=[(sem_V, 1)], drain=True)
            vc = sem_V.n
            C[("hamt_free", par)] = vc
            def f_sq(se, tt=t):
                return se.activation(sq_scr[:], xc_scr[:], AF.Square,
                                     accum_out=tiny[:, 2:3])
            p.emit("scalar", f_sq, wait=[(sem_V, vc)], inc=[(sem_S, 1)],
                   drain=True)
            C[("xc_free",)] = sem_S.n
            sc = sem_S.n
            def f_var(v, tt=t):
                v.tensor_scalar(tiny[:, 3:4], tiny[:, 2:3], 1.0 / D, 1e-5,
                                ALU.mult, ALU.add)
                v.drain()
                return v.reciprocal(tiny[:, 4:5], tiny[:, 3:4])
            p.emit("vector", f_var, wait=[(sem_S, sc)], inc=[(sem_V, 1)],
                   drain=True)
            vc2 = sem_V.n
            def f_rt(se, tt=t):
                return se.activation(sr1[:, tt:tt + 1], tiny[:, 4:5], AF.Sqrt)
            p.emit("scalar", f_rt, wait=[(sem_V, vc2)], inc=[(sem_S, 1)],
                   drain=True)
            sc2 = sem_S.n
            def f_apply(v, tt=t):
                v.tensor_scalar(xc_scr[:], xc_scr[:], sr1[:, tt:tt + 1],
                                None, ALU.mult)
                v.drain()
                v.tensor_tensor(xc_scr[:], xc_scr[:], l2w_sb[:], ALU.mult)
                v.drain()
                return v.tensor_tensor(xn_scr[:], xc_scr[:], l2b_sb[:],
                                       ALU.add)
            aw = [(sem_S, sc2)]
            if ("xn_free",) in C:
                aw.append((sem_T, C[("xn_free",)]))
            p.emit("vector", f_apply, wait=aw, inc=[(sem_V, 1)], drain=True)
            apply_done = sem_V.n
            for g in range(2):
                bk = g % 2
                tw = [(sem_V, apply_done)] if g == 0 else []
                if ("ptbank_free", bk) in C:
                    tw.append((sem_S, C[("ptbank_free", bk)]))
                for j in range(4):
                    c = g * 4 + j
                    def f_tr(cc=c, jj=j, bb=bk):
                        def f(te):
                            return nc.tensor.transpose(
                                PTR[bb][:, jj * 128:(jj + 1) * 128],
                                xn_scr[:, cc * 128:(cc + 1) * 128], id_sb[:])
                        return f
                    p.emit("tensor", f_tr(), wait=tw if j == 0 else [],
                           inc=[(sem_T, 1)] if j == 3 else [])
                    tw = []
                trc = sem_T.n
                def f_ev(gg=g, bb=bk, tt=t):
                    def f(se):
                        dst3 = ln2fm[:].rearrange("p (c t) -> p c t", t=T)[
                            :, gg * 4:(gg + 1) * 4, tt * 128:(tt + 1) * 128]
                        return se.copy(dst3, PTR[bb][:].rearrange(
                            "p (j t) -> p j t", t=128))
                    return f
                p.emit("scalar", f_ev(), wait=[(sem_T, trc)],
                       inc=[(sem_S, 1)], drain=True)
                C[("ptbank_free", bk)] = sem_S.n
            fm_done = sem_S.n
            C[("xn_free",)] = sem_T.n
            rw = [(sem_S, fm_done)]
            if ("pdn0_sig",) in C:
                rw.append((sem_S, C[("pdn0_sig",)]))
            if ("pdn_free", 0) in C:
                rw.append((sem_V, C[("pdn_free", 0)]))
            for c in range(ND):
                def f_rmm(cc=c, tt=t):
                    def f(te):
                        return nc.tensor.matmul(
                            P_dn[0][:, 0:8],
                            ln2fm[:, cc * T + tt * 128:cc * T + tt * 128 + 128],
                            rt_sb[:, cc * 8:(cc + 1) * 8],
                            start=(cc == 0), stop=(cc == ND - 1))
                    return f
                p.emit("tensor", f_rmm(), wait=rw if c == 0 else [],
                       inc=[(sem_T, 1)] if c == ND - 1 else [])
                rw = []
            rmm_done = sem_T.n
            def f_g1(v):
                v.reduce_max(tiny[:, 5:6], P_dn[0][:, 0:8], axis=AX.X)
                v.drain()
                return v.tensor_scalar(tiny[:, 6:7], tiny[:, 5:6], -1.0,
                                       None, ALU.mult)
            gw = [(sem_T, rmm_done)]
            if ("g8_free",) in C:
                gw.append((sem_V, C[("g8_free",)]))
            p.emit("vector", f_g1, wait=gw, inc=[(sem_V, 1)], drain=True)
            g1 = sem_V.n
            def f_g2(se):
                return se.activation(gate8[:], P_dn[0][:, 0:8], AF.Exp,
                                     bias=tiny[:, 6:7],
                                     accum_out=tiny[:, 7:8])
            p.emit("scalar", f_g2, wait=[(sem_V, g1)], inc=[(sem_S, 1)],
                   drain=True)
            C[("pdn0_sig",)] = sem_S.n
            g2 = sem_S.n
            def f_g3(v, tt=t):
                v.reciprocal(tiny[:, 0:1], tiny[:, 7:8])
                v.drain()
                v.tensor_scalar(gate8[:], gate8[:], tiny[:, 0:1], None,
                                ALU.mult)
                v.drain()
                v.tensor_copy(sel8[:], gate8[:])
                v.drain()
                v.reduce_max(tiny[:, 1:2], sel8[:], axis=AX.X)
                v.drain()
                v.tensor_scalar(mask8[:], sel8[:], tiny[:, 1:2], None,
                                ALU.is_ge)
                v.drain()
                v.scalar_tensor_tensor(sel8[:], mask8[:], NEG_BIG, sel8[:],
                                       ALU.mult, ALU.add)
                v.drain()
                v.reduce_max(tiny[:, 1:2], sel8[:], axis=AX.X)
                v.drain()
                v.tensor_scalar(mask8[:], gate8[:], tiny[:, 1:2], None,
                                ALU.is_ge)
                v.drain()
                return v.tensor_tensor(wts[:, tt * 8:(tt + 1) * 8],
                                       gate8[:], mask8[:], ALU.mult)
            p.emit("vector", f_g3, wait=[(sem_S, g2)], inc=[(sem_V, 1)],
                   drain=True)
            C[("g8_free",)] = sem_V.n
        ln2_all = sem_S.n
        wts_all = sem_V.n

        zw = [(sem_V, wts_all)]
        if ("yacc_free",) in C:
            zw.append((sem_xs[0], C[("yacc_free",)][0]))
            zw.append((sem_xs[1], C[("yacc_free",)][1]))
        p.emit("vector", lambda v: v.memset(yacc[:], 0.0), wait=zw,
               inc=[(sem_V, 1)], drain=True)
        yzero = sem_V.n

        for e in range(NE):
            w1w = [(sem_w1, sem_w1.n)]
            if ("w1_free",) in C:
                w1w.append((sem_T, C[("w1_free",)]))
            p.dma("gpsimd", w1_sb[:].rearrange("p (c o) -> p c o", o=F),
                  w1_all[e].rearrange("(c p) o -> p c o", p=128), sem_w1,
                  wait=[w for w in w1w if w[1] > 0])
            w1_ready = sem_w1.n
            w2w = [(sem_w2s, sem_w2s.n)]
            if ("w2_free",) in C:
                w2w.append((sem_T, C[("w2_free",)]))
            p.dma("gpsimd", w2_sb[:].rearrange("p (c o) -> p c o", o=D),
                  w2_all[e].rearrange("(c p) o -> p c o", p=128), sem_w2s,
                  wait=[w for w in w2w if w[1] > 0])
            w2_ready = sem_w2s.n

            first = True
            for tb in range(4):
                for fc in range(NF):
                    bk = fc % 2
                    mw = []
                    if first:
                        mw = [(sem_w1, w1_ready), (sem_S, ln2_all),
                              (sem_V, yzero)]
                        first = False
                    if ("ph_free", bk) in C:
                        mw.append((sem_S, C[("ph_free", bk)]))
                    for c in range(ND):
                        def f_h(cc=c, fcc=fc, bb=bk, tbb=tb):
                            def f(te):
                                return nc.tensor.matmul(
                                    P_h[bb][:],
                                    w1_sb[:, cc * F + fcc * 128:
                                          cc * F + fcc * 128 + 128],
                                    ln2fm[:, cc * T + tbb * 512:
                                          cc * T + (tbb + 1) * 512],
                                    start=(cc == 0), stop=(cc == ND - 1))
                            return f
                        p.emit("tensor", f_h(), wait=mw if c == 0 else [],
                               inc=[(sem_T, 1)] if c == ND - 1 else [])
                        mw = []
                    h_done = sem_T.n
                    gw2 = [(sem_T, h_done)]
                    if ("hblk_rd", fc) in C:
                        gw2.append((sem_T, C[("hblk_rd", fc)]))
                    def f_gelu(bb=bk, fcc=fc):
                        def f(se):
                            return se.activation(
                                hblk[:, fcc * 512:(fcc + 1) * 512],
                                P_h[bb][:], AF.Gelu_apprx_tanh)
                        return f
                    p.emit("scalar", f_gelu(), wait=gw2, inc=[(sem_S, 1)],
                           drain=True)
                    C[("ph_free", bk)] = sem_S.n
                h_all = sem_S.n
                for tt in range(4):
                    for ob in range(2):
                        bk = (tt * 2 + ob) % 2
                        gt = tb * 4 + tt
                        dw = [(sem_S, h_all)]
                        if tt == 0 and ob == 0:
                            dw.append((sem_w2s, w2_ready))
                        if e == 0 and tb == 0 and tt == 0 and ob == 0 \
                                and ("pdn0_sig",) in C:
                            dw.append((sem_S, C[("pdn0_sig",)]))
                        if ("pdn_free", bk) in C:
                            dw.append((sem_V, C[("pdn_free", bk)]))
                        for fc in range(NF):
                            def f_d(fcc=fc, ttt=tt, oo=ob, bb=bk):
                                def f(te):
                                    return nc.tensor.matmul(
                                        P_dn[bb][:],
                                        hblk[:, fcc * 512 + ttt * 128:
                                             fcc * 512 + ttt * 128 + 128],
                                        w2_sb[:, fcc * D + oo * 512:
                                              fcc * D + oo * 512 + 512],
                                        start=(fcc == 0), stop=(fcc == NF - 1))
                                return f
                            p.emit("tensor", f_d(), wait=dw if fc == 0 else [],
                                   inc=[(sem_T, 1)] if fc == NF - 1 else [])
                            dw = []
                        dn_done = sem_T.n
                        def f_acc(gtt=gt, oo=ob, bb=bk, ee=e):
                            def f(v):
                                ysl = yacc[:, gtt * D + oo * 512:
                                           gtt * D + (oo + 1) * 512]
                                return v.scalar_tensor_tensor(
                                    ysl, P_dn[bb][:],
                                    wts[:, gtt * 8 + ee:gtt * 8 + ee + 1],
                                    ysl, ALU.mult, ALU.add)
                            return f
                        p.emit("vector", f_acc(), wait=[(sem_T, dn_done)],
                               inc=[(sem_V, 1)], drain=True)
                        C[("pdn_free", bk)] = sem_V.n
                for fc in range(NF):
                    C[("hblk_rd", fc)] = sem_T.n
            C[("w1_free",)] = sem_T.n
            C[("w2_free",)] = sem_T.n
        acc_all = sem_V.n
        C[("moe_done",)] = sem_T.n

        for t in range(NT):
            par = t % 2
            hw = [(sem_V, C.get(("hamt_free", par), 0))]
            p.dma("sync", hamt[par][:], ham[s, t * 128:(t + 1) * 128, :],
                  sem_hl[par], wait=[w for w in hw if w[1] > 0])
            hrl = sem_hl[par].n
            lw = [(sem_V, C.get(("xtile_free", par), 0))]
            p.dma("sync", xtile[par][:], xin[s, t * 128:(t + 1) * 128, :],
                  sem_xl[par], wait=[w for w in lw if w[1] > 0])
            xrl = sem_xl[par].n
            def f_e1(v, tt=t, par_=par):
                ysl = yacc[:, tt * D:(tt + 1) * D]
                v.tensor_copy(xc_scr[:], hamt[par_][:])
                v.drain()
                v.tensor_tensor(ysl, ysl, xc_scr[:], ALU.add)
                v.drain()
                v.reduce_sum(ep_tiny[:, 0:1], ysl, axis=AX.X)
                v.drain()
                v.tensor_scalar(ep_tiny[:, 1:2], ep_tiny[:, 0:1], 1.0 / D,
                                None, ALU.mult)
                v.drain()
                return v.tensor_scalar(ysl, ysl, ep_tiny[:, 1:2], None,
                                       ALU.subtract)
            ew = [(sem_hl[par], hrl)]
            if t == 0:
                ew.append((sem_V, acc_all))
            if ("xc_free",) in C:
                ew.append((sem_S, C[("xc_free",)]))
            p.emit("vector", f_e1, wait=ew, inc=[(sem_V, 1)], drain=True)
            C[("hamt_free", par)] = sem_V.n
            e1 = sem_V.n
            def f_e2(se, tt=t):
                return se.activation(sq_scr[:], yacc[:, tt * D:(tt + 1) * D],
                                     AF.Square, accum_out=ep_tiny[:, 2:3])
            p.emit("scalar", f_e2, wait=[(sem_V, e1)], inc=[(sem_S, 1)],
                   drain=True)
            C[("xc_free",)] = sem_S.n
            e2 = sem_S.n
            def f_e3(v):
                v.tensor_scalar(ep_tiny[:, 3:4], ep_tiny[:, 2:3], 1.0 / D,
                                1e-5, ALU.mult, ALU.add)
                v.drain()
                return v.reciprocal(ep_tiny[:, 4:5], ep_tiny[:, 3:4])
            p.emit("vector", f_e3, wait=[(sem_S, e2)], inc=[(sem_V, 1)],
                   drain=True)
            e3 = sem_V.n
            def f_e4(se):
                return se.activation(ep_tiny[:, 5:6], ep_tiny[:, 4:5], AF.Sqrt)
            p.emit("scalar", f_e4, wait=[(sem_V, e3)], inc=[(sem_S, 1)],
                   drain=True)
            e4 = sem_S.n
            def f_e5(par_=par, tt=t):
                def f(v):
                    ysl = yacc[:, tt * D:(tt + 1) * D]
                    v.tensor_scalar(ysl, ysl, ep_tiny[:, 5:6], None, ALU.mult)
                    v.drain()
                    return v.tensor_tensor(ysl, xtile[par_][:], ysl, ALU.add)
                return f
            p.emit("vector", f_e5(), wait=[(sem_S, e4), (sem_xl[par], xrl)],
                   inc=[(sem_V, 1)], drain=True)
            C[("xtile_free", par)] = sem_V.n
            outd = sem_V.n
            p.dma("sync", xout[s, t * 128:(t + 1) * 128, :],
                  yacc[:, t * D:(t + 1) * D], sem_xs[par],
                  wait=[(sem_V, outd)])
            C[("zst_free", par)] = sem_xs[par].n
        C[("yacc_free",)] = (sem_xs[0].n, sem_xs[1].n)

    p.emit("sync", lambda h: None,
           wait=[(sem_xs[0], sem_xs[0].n), (sem_xs[1], sem_xs[1].n)])
    p.build()
    es.close()
    return nc


def _l1ffn_feeds(inputs, xs, q=None, k_full=None, v_full=None, ham=None,
                 which="a"):
    cst = host_consts()
    if which == "a":
        common = {
            "wqkv": np.ascontiguousarray(inputs["L1_ffn_qkv_weight"].T),
            "l1w": np.ascontiguousarray(
                inputs["L1_ffn_ln1_weight"].reshape(1, D)),
            "l1b": np.ascontiguousarray(
                inputs["L1_ffn_ln1_bias"].reshape(1, D)),
            "ident": cst["ident"],
        }
        return [dict(common, xin=np.ascontiguousarray(x)) for x in xs]
    if which == "b1":
        common = {
            "k_all": np.ascontiguousarray(k_full),
            "v_all": np.ascontiguousarray(v_full),
            "projT": np.ascontiguousarray(inputs["L1_ffn_proj_weight"].T),
            "ident": cst["ident"],
        }
        feeds = []
        for c in range(NCORE):
            m = np.full((NSEQ, 32), NEG_BIG, np.float32)
            for sl in range(NSEQ):
                sg = c + 8 * sl
                m[sl, :sg + 1] = 0.0
            feeds.append(dict(common, xin=np.ascontiguousarray(xs[c]),
                              q_in=np.ascontiguousarray(q[c]),
                              masks=m))
        return feeds
    if which == "b2":
        common = {
            "w1_all": np.ascontiguousarray(inputs["L1_ffn_moe_w1"]),
            "w2_all": np.ascontiguousarray(inputs["L1_ffn_moe_w2"]),
            "routT": np.ascontiguousarray(inputs["L1_ffn_router_weight"].T),
            "l2w": np.ascontiguousarray(
                inputs["L1_ffn_ln2_weight"].reshape(1, D)),
            "l2b": np.ascontiguousarray(
                inputs["L1_ffn_ln2_bias"].reshape(1, D)),
            "ident": cst["ident"],
        }
        return [dict(common, xin=np.ascontiguousarray(xs[c]),
                     ham=np.ascontiguousarray(ham[c]))
                for c in range(NCORE)]


def run_l1ffn(inputs, x, nseq=NSEQ):
    xs = [x[c::NCORE][:nseq] for c in range(NCORE)]
    nc = build_l1ffn_a(nseq)
    res = run_bass_kernel_spmd(nc, _l1ffn_feeds(inputs, xs, which="a"),
                               core_ids=list(range(NCORE)))
    qs = [res.results[c]["q_out"] for c in range(NCORE)]
    ks = [res.results[c]["k_out"] for c in range(NCORE)]
    vs = [res.results[c]["v_out"] for c in range(NCORE)]
    k_full = np.stack([ks[sg % 8][sg // 8] for sg in range(32)], 0)
    v_full = np.stack([vs[sg % 8][sg // 8] for sg in range(32)], 0)
    nc = build_l1ffn_b1(nseq)
    res = run_bass_kernel_spmd(
        nc, _l1ffn_feeds(inputs, xs, q=qs, k_full=k_full, v_full=v_full,
                         which="b1"), core_ids=list(range(NCORE)))
    hams = [res.results[c]["ham"] for c in range(NCORE)]
    nc = build_l1ffn_b2(nseq)
    res = run_bass_kernel_spmd(
        nc, _l1ffn_feeds(inputs, xs, ham=hams, which="b2"),
        core_ids=list(range(NCORE)))
    out = np.empty_like(x[:nseq * NCORE])
    for c in range(NCORE):
        out[c::NCORE] = res.results[c]["xout"]
    return out



def _prep_l0attn_feeds(inputs, xs):
    cst = host_consts()
    wq, wk, wv = inputs["L0_attn_Wq"], inputs["L0_attn_Wk"], inputs["L0_attn_Wv"]
    wqkv = np.ascontiguousarray(
        np.concatenate([wq.T, wk.T, wv.T], axis=1)).astype(np.float32)
    bqkv = np.concatenate([inputs["L0_attn_bq"], inputs["L0_attn_bk"],
                           inputs["L0_attn_bv"]]).reshape(1, 3 * D)
    common = {
        "wqkv": wqkv,
        "bqkv": np.ascontiguousarray(bqkv),
        "wo": np.ascontiguousarray(inputs["L0_attn_Wo"].T),
        "bo": np.ascontiguousarray(inputs["L0_attn_bo"].reshape(1, D)),
        "lnw": np.ascontiguousarray(inputs["L0_attn_ln_w"].reshape(1, D)),
        "lnb": np.ascontiguousarray(inputs["L0_attn_ln_b"].reshape(1, D)),
        "ident": cst["ident"],
        "ones_row": cst["ones_row"],
        "cos_l": cst["cos_l"],
        "sin_l": cst["sin_l"],
    }
    return [dict(common, xin=np.ascontiguousarray(xs[c])) for c in range(NCORE)]


def run_l0attn(inputs, x, nseq=NSEQ):
    xs = [x[c::NCORE][:nseq] for c in range(NCORE)]
    nc = build_l0attn(nseq)
    feeds = _prep_l0attn_feeds(inputs, xs)
    res = run_bass_kernel_spmd(nc, feeds, core_ids=list(range(NCORE)))
    out = np.empty_like(x[:nseq * NCORE])
    for c in range(NCORE):
        out[c::NCORE] = res.results[c]["xout"]
    return out


def kernel(**inputs):
    x = np.ascontiguousarray(inputs["L0_attn_x"].astype(np.float32))
    x = run_l0attn(inputs, x)
    x = run_l0ffn(inputs, x)
    x = run_l1attn(inputs, x)
    x = run_l1ffn(inputs, x)
    return x
